# version 02
# baseline (speedup 1.0000x reference)
from contextlib import ExitStack

import numpy as np
import ml_dtypes
import concourse.bass as bass
import concourse.mybir as mybir
from concourse.bass_utils import run_bass_kernel_spmd

F32 = mybir.dt.float32
BF16 = mybir.dt.bfloat16
AF = mybir.ActivationFunctionType
ALU = mybir.AluOpType
AX = mybir.AxisListType
bf16 = ml_dtypes.bfloat16

D = 2048
T = 1024
NT = 8
NCH = 16
E = 16
MI = 1024
SI = 4096
NEG = -30000.0
EPS = 1e-5

def core_tiles(c):
    return [c, 8 + c, 16 + c, 24 + c, 39 - c, 47 - c, 55 - c, 63 - c]

def g_of(c, s):
    return 8 * s + c if s < 4 else 8 * s + 7 - c

LF = [8 * s + 7 for s in range(8)]


class Prog:

    KEYS = ("sy", "t", "v", "s", "g")
    CHAINED = ("v", "s", "g")

    def __init__(self, nc, stack):
        self.nc = nc
        self.stack = stack
        self.q = {k: [] for k in self.KEYS}
        self.sems = {}
        self.cnt = {k: 0 for k in self.CHAINED}
        self.virt = {}
        for k in self.CHAINED:
            self.sem("self_" + k)

    def sem(self, name):
        if name not in self.sems:
            h = self.stack.enter_context(self.nc.semaphore(name))
            self.sems[name] = [h, 0]
        return name

    def n(self, name):
        if name in self.virt:
            return len(self.virt[name][1])
        return self.sems[self.sem(name)][1]

    def _xlate(self, nm, v):
        v = int(v)
        if nm in self.virt:
            eng, lst = self.virt[nm]
            return ("self_" + eng, 0 if v <= 0 else lst[min(v, len(lst)) - 1])
        return (self.sem(nm), v)

    def emit(self, eng, build, waits=(), incs=(), dma=False):
        ws = [self._xlate(nm, v) for nm, v in waits]
        real = []
        pos = None
        chained = (eng in self.CHAINED) and not dma
        if build is not None and chained:
            self.cnt[eng] += 1
            pos = self.cnt[eng]
        for nm, k in incs:
            if chained:
                assert build is not None and k == 1, (eng, nm, k)
                ent = self.virt.setdefault(nm, (eng, []))
                assert ent[0] == eng, (nm, ent[0], eng)
                ent[1].append(pos)
            else:
                assert nm not in self.virt, nm
                self.sem(nm)
                self.sems[nm][1] += k
                real.append((nm, k))
        assert len(real) <= 1, (eng, incs)
        self.q[eng].append((build, ws, real, pos))

    def wait(self, eng, name, val=None):
        self.emit(eng, None, waits=[(name, self.n(name) if val is None else val)])

    def play(self, name):
        attr = {"sy": "sync", "t": "tensor", "v": "vector", "s": "scalar", "g": "gpsimd"}
        with self.nc.Block(name) as blk:
            for key in self.KEYS:
                ops = self.q[key]
                if not ops:
                    continue
                chain = key in self.CHAINED
                snm = ("self_" + key) if chain else None

                def body(eng, ops=ops, chain=chain, snm=snm):
                    for build, ws, real, pos in ops:
                        for nm, v in ws:
                            eng.wait_ge(self.sems[nm][0], v)
                        if build is None:
                            continue
                        if chain and pos is not None:
                            eng.wait_ge(self.sems[snm][0], pos - 1)
                        inst = build(eng)
                        if pos is not None:
                            inst.then_inc(self.sems[snm][0], 1)
                            self.sems[snm][1] += 1
                        else:
                            for nm, k in real:
                                inst.then_inc(self.sems[nm][0], k)

                getattr(blk, attr[key])(body)
        self.q = {k: [] for k in self.KEYS}


def build(debug=False):
    nc = bass.Bass("TRN2", target_bir_lowering=False)
    dt = nc.dram_tensor

    x_in = dt("x0", [T, D], F32, kind="ExternalInput").ap()
    wqt = dt("wqt", [D, D], F32, kind="ExternalInput").ap()
    wkt = dt("wkt", [D, 512], F32, kind="ExternalInput").ap()
    wvt = dt("wvt", [D, 512], F32, kind="ExternalInput").ap()
    wot = dt("wot", [D, D], F32, kind="ExternalInput").ap()
    rwt = dt("rwt", [D, 16], F32, kind="ExternalInput").ap()
    ebias_d = dt("ebias", [1, 16], F32, kind="ExternalInput").ap()
    wup = dt("wup", [E, D, MI], F32, kind="ExternalInput").ap()
    wdn = dt("wdn", [E, MI, D], F32, kind="ExternalInput").ap()
    su_d = dt("su", [D, SI], F32, kind="ExternalInput").ap()
    sd_d = dt("sd", [SI, D], F32, kind="ExternalInput").ap()
    c_idbf = dt("c_idbf", [128, 128], BF16, kind="ExternalInput").ap()
    c_idf = dt("c_idf", [128, 128], F32, kind="ExternalInput").ap()
    c_ones = dt("c_ones", [128, 128], F32, kind="ExternalInput").ap()
    c_iota16 = dt("c_iota16", [128, 16], F32, kind="ExternalInput").ap()
    c_iota4 = dt("c_iota4", [128, 4], F32, kind="ExternalInput").ap()
    c_i16m = dt("c_i16m", [128, 16], F32, kind="ExternalInput").ap()
    c_i4m = dt("c_i4m", [128, 4], F32, kind="ExternalInput").ap()
    c_onesb = dt("c_onesb", [128, 1], BF16, kind="ExternalInput").ap()
    c_eyeexp = dt("c_eyeexp", [16, E * 128], F32, kind="ExternalInput").ap()
    c_tril = dt("c_tril", [128, 512], F32, kind="ExternalInput").ap()
    c_pad = dt("c_pad", [1, 8 * 65], F32, kind="ExternalInput").ap()

    out_d = dt("out", [T, D], F32, kind="ExternalOutput").ap()

    xbuf = dt("xbuf", [T, D], F32).ap()
    qdr = dt("qdr", [4, 128, 4096], BF16).ap()
    kag_i = dt("kag_i", [512, 1024], BF16).ap()
    kag_o = dt("kag_o", [4096, 1024], BF16).ap()
    vag_i = dt("vag_i", [1024, 520], BF16).ap()
    vag_o = dt("vag_o", [8192, 520], BF16).ap()
    nrm_i = dt("nrm_i", [1, 8], F32).ap()
    nrm_o = dt("nrm_o", [8, 8], F32).ap()

    root = ExitStack()

    def SB(st, name, shape, d):
        return st.enter_context(nc.sbuf_tensor(name, shape, d))

    def PS(st, name, shape, d):
        return st.enter_context(nc.psum_tensor(name, shape, d))

    with root:
        yfm = SB(root, "yfm", [128, NCH * T], BF16)
        xt = SB(root, "xt", [128, D], F32)
        yscr = SB(root, "yscr", [128, D], F32)
        ybf = SB(root, "ybf", [128, D], BF16)
        idbf = SB(root, "idbf", [128, 128], BF16)
        idf = SB(root, "idf", [128, 128], F32)
        onesf = SB(root, "onesf", [128, 128], F32)
        onesb = SB(root, "onesb", [128, 1], BF16)
        iota16 = SB(root, "iota16", [128, 16], F32)
        iota4 = SB(root, "iota4", [128, 4], F32)
        i16m = SB(root, "i16m", [128, 16], F32)
        i4m = SB(root, "i4m", [128, 4], F32)
        rwts = SB(root, "rwts", [128, 16 * 16], F32)
        ebb = SB(root, "ebb", [128, 16], F32)
        ebrow = SB(root, "ebrow", [1, 16], F32)
        cwT = SB(root, "cwT", [16, T], F32)
        rsc = SB(root, "rsc", [128, 2 * 128], F32)
        col = SB(root, "col", [128, 16], F32)
        tk = SB(root, "tk", [128, 12 * 16], F32)
        tk4 = SB(root, "tk4", [128, 16 * 4], F32)
        c4 = SB(root, "c4", [128, 4], F32)
        c16s = SB(root, "c16s", [128, 16], F32)
        neg30 = SB(root, "neg30", [128, 16], F32)
        trils = SB(root, "trils", [128, 512], F32)
        pads = SB(root, "pads", [1, 8 * 65], F32)
        epscol = SB(root, "epscol", [128, 1], F32)
        stf = [SB(root, "stf%d" % i, [128, 2048], F32) for i in range(2)]
        stb = [SB(root, "stb%d" % i, [128, 4096], BF16) for i in range(2)]


        P = Prog(nc, root)
        em = P.emit

        def wstream(b, parts, wait_extra=(), dst=None, guard=True):
            scb = "sc%d" % b
            dmb = "dmab%d" % b
            dstt = stb[b] if dst is None else dst
            for j, (srcap, off, w) in enumerate(parts):
                em("sy", lambda e, srcap=srcap, w=w, b=b: e.dma_start(
                    stf[b][:, 0:w].rearrange("p (k c) -> p k c", c=128), srcap),
                   waits=[(scb, P.n(scb))] + (list(wait_extra) if j == 0 else []),
                   incs=[(dmb, 16)])
                ws = [(dmb, P.n(dmb))]
                if j == 0 and guard:
                    ws.append(("pT", max(0, P.n("pT") - 1)))
                em("v", lambda e, off=off, w=w, dstt=dstt, b=b: e.tensor_copy(
                    dstt[:, off:off + w], stf[b][:, 0:w]),
                   waits=ws, incs=[(scb, 1)])
            return (scb, P.n(scb))

        for dst, src in [(idbf, c_idbf), (idf, c_idf), (onesf, c_ones),
                         (iota16, c_iota16), (iota4, c_iota4), (trils, c_tril),
                         (i16m, c_i16m), (i4m, c_i4m), (onesb, c_onesb),
                         (pads, c_pad), (ebrow, ebias_d)]:
            em("sy", lambda e, a=dst, b=src: e.dma_start(a[:], b), incs=[("dcn", 16)])
        em("sy", lambda e: e.dma_start(rwts[:].rearrange("p (k e) -> p k e", e=16),
                                       rwt.rearrange("(k p) e -> p k e", p=128)),
           incs=[("dcn", 16)])
        em("v", lambda e: e.memset(neg30[:], -1e30))
        em("v", lambda e: e.memset(epscol[:], EPS))
        em("v", lambda e: e.memset(c4[:], 4.0))
        em("v", lambda e: e.memset(c16s[:], 16.0))
        with ExitStack() as cS:
            pcn = PS(cS, "pcn", [128, 512], F32)
            em("t", lambda e, pcn=pcn: e.matmul(pcn[:, 0:16], onesf[0:1, 0:128],
                                                ebrow[0:1, 0:16], start=True, stop=True),
               waits=[("dcn", P.n("dcn"))], incs=[("pT", 1)])
            em("v", lambda e, pcn=pcn: e.tensor_copy(ebb[:], pcn[:, 0:16]),
               waits=[("pT", P.n("pT"))])
            P.wait("sy", "dcn")
            P.play("consts")

        def topk_emit(ti, lp):
            S_ = lambda i: tk[:, i * 16:(i + 1) * 16]
            G_ = lambda i: tk4[:, i * 4:(i + 1) * 4]
            C_ = lambda i: col[:, 8 + i:9 + i]
            lg, tf = lp["lg"], lp["tf"]
            em("s", lambda e, lg=lg: e.activation(S_(0), lg[:, 0:16], AF.Sigmoid),
               waits=[("pT", P.n("pT"))], incs=[("pS", 1)])
            em("v", lambda e: e.tensor_add(S_(1), S_(0), ebb[:]),
               waits=[("pS", P.n("pS"))])
            sv = lambda j: S_(1).rearrange("p (g j) -> p g j", j=4)[:, :, j:j + 1]
            em("v", lambda e: e.tensor_max(G_(0), sv(0), sv(1)))
            em("v", lambda e: e.tensor_max(G_(1), sv(2), sv(3)))
            em("v", lambda e: e.tensor_add(G_(2), sv(0), sv(1)))
            em("v", lambda e: e.tensor_add(G_(3), sv(2), sv(3)))
            em("v", lambda e: e.tensor_add(G_(0), G_(0), G_(1)))
            em("v", lambda e: e.tensor_max(G_(0), G_(0), G_(2)))
            em("v", lambda e: e.tensor_max(G_(0), G_(0), G_(3)))
            em("v", lambda e: e.tensor_reduce(C_(0), G_(0), AX.X, ALU.max))
            em("v", lambda e: e.tensor_single_scalar(G_(1), G_(0), C_(0), ALU.is_ge))
            em("v", lambda e: e.tensor_mul(G_(2), G_(1), i4m[:]))
            em("v", lambda e: e.tensor_add(G_(2), G_(2), c4[:]))
            em("v", lambda e: e.tensor_reduce(C_(1), G_(2), AX.X, ALU.min))
            em("v", lambda e: e.tensor_single_scalar(G_(1), iota4[:], C_(1), ALU.is_equal))
            em("v", lambda e: e.tensor_mul(G_(3), G_(1), neg30[:, 0:4]))
            em("v", lambda e: e.tensor_add(G_(3), G_(3), G_(0)))
            em("v", lambda e: e.tensor_reduce(C_(0), G_(3), AX.X, ALU.max))
            em("v", lambda e: e.tensor_single_scalar(G_(2), G_(3), C_(0), ALU.is_ge))
            em("v", lambda e: e.tensor_mul(G_(2), G_(2), i4m[:]))
            em("v", lambda e: e.tensor_add(G_(2), G_(2), c4[:]))
            em("v", lambda e: e.tensor_reduce(C_(1), G_(2), AX.X, ALU.min))
            em("v", lambda e: e.tensor_single_scalar(G_(2), iota4[:], C_(1), ALU.is_equal))
            em("v", lambda e: e.tensor_add(G_(0), G_(1), G_(2)))
            for j in range(4):
                em("v", lambda e, j=j: e.tensor_copy(
                    S_(2).rearrange("p (g j) -> p g j", j=4)[:, :, j:j + 1], G_(0)))
            em("v", lambda e: e.tensor_mul(S_(3), S_(1), S_(2)))
            em("v", lambda e: e.tensor_reduce(C_(0), S_(3), AX.X, ALU.max))
            em("v", lambda e: e.tensor_single_scalar(S_(5), S_(3), C_(0), ALU.is_ge))
            em("v", lambda e: e.tensor_mul(S_(5), S_(5), i16m[:]))
            em("v", lambda e: e.tensor_add(S_(5), S_(5), c16s[:]))
            em("v", lambda e: e.tensor_reduce(C_(1), S_(5), AX.X, ALU.min))
            em("v", lambda e: e.tensor_single_scalar(S_(6), iota16[:], C_(1), ALU.is_equal))
            em("v", lambda e: e.tensor_mul(S_(4), S_(6), neg30[:]))
            em("v", lambda e: e.tensor_add(S_(4), S_(4), S_(3)))
            em("v", lambda e: e.tensor_reduce(C_(0), S_(4), AX.X, ALU.max))
            em("v", lambda e: e.tensor_single_scalar(S_(5), S_(4), C_(0), ALU.is_ge))
            em("v", lambda e: e.tensor_mul(S_(5), S_(5), i16m[:]))
            em("v", lambda e: e.tensor_add(S_(5), S_(5), c16s[:]))
            em("v", lambda e: e.tensor_reduce(C_(1), S_(5), AX.X, ALU.min))
            em("v", lambda e: e.tensor_single_scalar(S_(7), iota16[:], C_(1), ALU.is_equal))
            em("v", lambda e: e.tensor_mul(S_(5), S_(0), S_(6)))
            em("v", lambda e: e.tensor_reduce(C_(2), S_(5), AX.X, ALU.add))
            em("v", lambda e: e.tensor_mul(S_(5), S_(0), S_(7)))
            em("v", lambda e: e.tensor_reduce(C_(3), S_(5), AX.X, ALU.add))
            em("v", lambda e: e.tensor_add(C_(4), C_(2), C_(3)))
            em("v", lambda e: e.tensor_scalar_add(C_(4), C_(4), 1e-20))
            em("v", lambda e: e.reciprocal(C_(5), C_(4)))
            em("v", lambda e: e.tensor_mul(C_(2), C_(2), C_(5)))
            em("v", lambda e: e.tensor_mul(C_(3), C_(3), C_(5)))
            em("v", lambda e: e.tensor_single_scalar(S_(8), S_(6), C_(2), ALU.mult))
            em("v", lambda e: e.scalar_tensor_tensor(S_(8), S_(7), C_(3), S_(8),
                                                     ALU.mult, ALU.add),
               incs=[("pV", 1)])
            em("t", lambda e, tf=tf: e.transpose(tf[0:16, 0:128], S_(8), idf[:]),
               waits=[("pV", P.n("pV")), ("a2r", P.n("a2r"))], incs=[("pT", 1)])
            em("v", lambda e, ti=ti, tf=tf: e.tensor_copy(
                cwT[0:16, ti * 128:(ti + 1) * 128], tf[0:16, 0:128]),
               waits=[("pT", P.n("pT"))], incs=[("cwe", 1)])

        def emit_ln_core(ti, do_router, lp):
            em("v", lambda e: e.tensor_reduce(col[:, 0:1], xt[:], AX.X, ALU.add),
               waits=[("dxt", P.n("dxt"))])
            em("s", lambda e: e.activation(ybf[:], xt[:], AF.Square,
                                           accum_out=col[:, 1:2]),
               waits=[("dxt", P.n("dxt")), ("pV", P.n("pV"))], incs=[("pS", 1)])
            em("v", lambda e: e.tensor_scalar_mul(col[:, 2:3], col[:, 0:1], 1.0 / D))
            em("v", lambda e: e.tensor_mul(col[:, 3:4], col[:, 2:3], col[:, 2:3]),
               waits=[("pS", P.n("pS"))])
            em("v", lambda e: e.tensor_scalar_mul(col[:, 4:5], col[:, 1:2], 1.0 / D))
            em("v", lambda e: e.tensor_sub(col[:, 4:5], col[:, 4:5], col[:, 3:4]),
               incs=[("pV", 1)])
            em("s", lambda e: e.activation(col[:, 5:6], col[:, 4:5], AF.Sqrt,
                                           bias=epscol[:, 0:1]),
               waits=[("pV", P.n("pV"))], incs=[("pS", 1)])
            em("v", lambda e: e.reciprocal(col[:, 6:7], col[:, 5:6]),
               waits=[("pS", P.n("pS"))])
            em("v", lambda e: e.tensor_scalar_mul(col[:, 7:8], col[:, 2:3], -1.0))
            em("v", lambda e: e.tensor_scalar(yscr[:], xt[:], col[:, 7:8],
                                              col[:, 6:7], ALU.add, ALU.mult))
            em("v", lambda e: e.tensor_copy(ybf[:], yscr[:]), incs=[("pV", 1)])
            for c in range(NCH):
                pp = lp["ga"] if (c % 2 == 0) else lp["gb"]
                em("t", lambda e, c=c, pp=pp: e.transpose(
                    pp[:, 0:128], ybf[:, c * 128:(c + 1) * 128], idbf[:]),
                   waits=[("pV", P.n("pV")), ("a2", max(0, P.n("a2") - 1))],
                   incs=[("a1", 1)])
                em("v", lambda e, c=c, ti=ti, pp=pp: e.tensor_copy(
                    yfm[:, c * T + ti * 128: c * T + (ti + 1) * 128], pp[:, 0:128]),
                   waits=[("a1", P.n("a1"))], incs=[("a2", 1)])
            if do_router:
                tf = lp["tf"]
                for c in range(NCH):
                    pp = tf[:, 0:128] if (c % 2 == 0) else tf[:, 512:640]
                    sl = rsc[:, (c % 2) * 128:((c % 2) + 1) * 128]
                    em("t", lambda e, c=c, pp=pp: e.transpose(
                        pp, yscr[:, c * 128:(c + 1) * 128], idf[:]),
                       waits=[("a2r", max(0, P.n("a2r") - 1))], incs=[("a1r", 1)])
                    em("v", lambda e, sl=sl, pp=pp: e.tensor_copy(sl, pp),
                       waits=[("a1r", P.n("a1r"))], incs=[("a2r", 1)])
                    w0 = [("a2r", P.n("a2r"))]
                    if c == 0:
                        w0.append(("pS", P.n("pS")))
                    em("t", lambda e, c=c, sl=sl: e.matmul(
                        lp["lg"][:, 0:16], sl, rwts[:, c * 16:(c + 1) * 16],
                        start=(c == 0), stop=(c == NCH - 1)),
                       waits=w0, incs=([("pT", 1)] if c == NCH - 1 else []))
                topk_emit(ti, lp)

        def ln_phase(src_fn, dst_dram, do_router, add_facc, facc_t, name,
                     store_out_only=False):
            lnS = ExitStack()
            lp = {"ga": PS(lnS, name + "_ga", [128, 1024], BF16),
                  "gb": PS(lnS, name + "_gb", [128, 1024], BF16),
                  "tf": PS(lnS, name + "_tf", [128, 1024], F32),
                  "lg": PS(lnS, name + "_lg", [128, 512], F32)}
            for ti in range(NT):
                em("sy", lambda e, ti=ti: e.dma_start(xt[:], src_fn(ti)),
                   waits=[("pV", P.n("pV")), ("pS", P.n("pS")), ("dxt", P.n("dxt"))],
                   incs=[("dxt", 16)])
                if add_facc:
                    for c in range(NCH):
                        pp = lp["tf"][:, 0:128] if (c % 2 == 0) else lp["tf"][:, 512:640]
                        em("t", lambda e, c=c, ti=ti, pp=pp: e.transpose(
                            pp,
                            facc_t[:, c * T + ti * 128: c * T + (ti + 1) * 128], idf[:]),
                           waits=[("pV", P.n("pV"))], incs=[("pT", 1)])
                        em("v", lambda e, c=c, pp=pp: e.scalar_tensor_tensor(
                            xt[:, c * 128:(c + 1) * 128], pp, 0.5,
                            xt[:, c * 128:(c + 1) * 128], ALU.mult, ALU.add),
                           waits=[("pT", P.n("pT")), ("dxt", P.n("dxt"))], incs=[("pV", 1)])
                if not store_out_only:
                    emit_ln_core(ti, do_router, lp)
                if dst_dram is not None:
                    em("sy", lambda e, ti=ti: e.dma_start(
                        dst_dram[ti * 128:(ti + 1) * 128, :], xt[:]),
                       waits=[("pV", P.n("pV")), ("pS", P.n("pS"))], incs=[("dxt", 16)])
            P.wait("sy", "dxt")
            P.play(name)
            lnS.close()

        def shared_phase(facc_t, hab, name):
            shS = ExitStack()
            pU = [PS(shS, name + "_pU%d" % i, [128, 1024], F32) for i in range(2)]
            pDn = [PS(shS, name + "_pDn%d" % i, [128, 512], F32) for i in range(2)]
            hrl = [name_scr["hrl"], name_scr["hrl2"]]
            for th in range(2):
                to = th * 512
                for m in range(32):
                    b = m % 2
                    pp = pU[b]
                    hh = hrl[b]
                    wcd = wstream(b, [(su_d[:, m * 128:(m + 1) * 128]
                                       .rearrange("(k p) c -> p k c", p=128), 0, 2048)])
                    for k in range(NCH):
                        w0 = [wcd, ("rl", max(0, P.n("rl") - 1))] if k == 0 else []
                        em("t", lambda e, b=b, k=k, to=to, pp=pp: e.matmul(
                            pp[:, 0:512], stb[b][:, k * 128:(k + 1) * 128],
                            yfm[:, k * T + to: k * T + to + 512],
                            start=(k == 0), stop=(k == NCH - 1)),
                           waits=w0, incs=([("pT", 1)] if k == NCH - 1 else []))
                    em("s", lambda e, pp=pp, hh=hh: e.activation(hh[:, 0:512],
                                                                 pp[:, 0:512], AF.Relu),
                       waits=[("pT", P.n("pT")), ("sq", max(0, P.n("sq") - 1))],
                       incs=[("rl", 1)])
                    wm = [("rl", P.n("rl"))]
                    if th == 1 and m == 0:
                        wm.append(("pT", P.n("pT")))
                    em("v", lambda e, m=m, hh=hh: e.tensor_mul(
                        hab[:, m * 512:(m + 1) * 512], hh[:, 0:512], hh[:, 0:512]),
                       waits=wm, incs=[("sq", 1)])
                for mf in range(NCH):
                    b = mf % 2
                    pp = pDn[b]
                    wx = [("sq", P.n("sq"))] if mf == 0 else []
                    wcd = wstream(b, [(sd_d[0:2048, mf * 128:(mf + 1) * 128]
                                       .rearrange("(k p) c -> p k c", p=128), 0, 2048),
                                      (sd_d[2048:4096, mf * 128:(mf + 1) * 128]
                                       .rearrange("(k p) c -> p k c", p=128), 2048, 2048)],
                                  wait_extra=wx)
                    for k in range(32):
                        w0 = [wcd, ("fadd", max(0, P.n("fadd") - 1))] if k == 0 else []
                        if k == 0 and mf == 0:
                            w0.append(("sq", P.n("sq")))
                        em("t", lambda e, b=b, k=k, pp=pp: e.matmul(
                            pp[:], stb[b][:, k * 128:(k + 1) * 128],
                            hab[:, k * 512:(k + 1) * 512],
                            start=(k == 0), stop=(k == 31)),
                           waits=w0, incs=([("pT", 1)] if k == 31 else []))
                    em("v", lambda e, mf=mf, to=to, pp=pp: e.scalar_tensor_tensor(
                        facc_t[:, mf * T + to: mf * T + to + 512], pp[:], 1.0,
                        facc_t[:, mf * T + to: mf * T + to + 512], ALU.mult, ALU.add),
                       waits=[("pT", P.n("pT"))], incs=[("fadd", 1)])
            P.wait("sy", "dmab0")
            P.wait("sy", "dmab1")
            P.play(name)
            shS.close()

        def moe_phase(facc_t, hab, name):
            moS = ExitStack()
            pU = [PS(moS, name + "_pU%d" % i, [128, 1024], F32) for i in range(2)]
            pDn = [PS(moS, name + "_pDn%d" % i, [128, 1024], F32) for i in range(2)]
            hrl = [name_scr["hrl"], name_scr["hrl2"]]
            cwb = name_scr["cwb"]
            eyeexp = name_scr["eyeexp"]
            ae = hab
            for ex in range(E):
                for half in range(2):
                    em("t", lambda e, ex=ex, half=half: e.matmul(
                        pDn[0][:, 0:512], eyeexp[:, ex * 128:(ex + 1) * 128],
                        cwT[:, half * 512:(half + 1) * 512], start=True, stop=True),
                       waits=[("cwcp", P.n("cwcp")), ("cwe", P.n("cwe")),
                              ("fadd", P.n("fadd"))],
                       incs=[("a1", 1)])
                    em("v", lambda e, half=half: e.tensor_copy(
                        cwb[:, half * 512:(half + 1) * 512], pDn[0][:, 0:512]),
                       waits=[("a1", P.n("a1"))], incs=[("cwcp", 1)])
                for m in range(8):
                    b = m % 2
                    pp = pU[b]
                    hh = hrl[b]
                    wcd = wstream(b, [(wup[ex, :, m * 128:(m + 1) * 128]
                                       .rearrange("(k p) c -> p k c", p=128), 0, 2048)])
                    for k in range(NCH):
                        w0 = [wcd, ("rl", max(0, P.n("rl") - 1))] if k == 0 else []
                        em("t", lambda e, b=b, k=k, pp=pp: e.matmul(
                            pp[:, 0:512], stb[b][:, k * 128:(k + 1) * 128],
                            yfm[:, k * T: k * T + 512],
                            start=(k == 0), stop=(k == NCH - 1)), waits=w0)
                        em("t", lambda e, b=b, k=k, pp=pp: e.matmul(
                            pp[:, 512:1024], stb[b][:, k * 128:(k + 1) * 128],
                            yfm[:, k * T + 512: k * T + 1024],
                            start=(k == 0), stop=(k == NCH - 1)),
                           incs=([("pT", 1)] if k == NCH - 1 else []))
                    em("s", lambda e, pp=pp, hh=hh: e.activation(hh[:], pp[:],
                                                                 AF.Relu),
                       waits=[("pT", P.n("pT")), ("sq", max(0, P.n("sq") - 1))],
                       incs=[("rl", 1)])
                    em("v", lambda e, hh=hh: e.tensor_mul(hh[:], hh[:], hh[:]),
                       waits=[("rl", P.n("rl")), ("cwcp", P.n("cwcp"))])
                    wm = [("pT", P.n("pT"))] if m == 0 else []
                    em("v", lambda e, m=m, hh=hh: e.tensor_mul(
                        ae[:, m * T:(m + 1) * T], hh[:], cwb[:]),
                       waits=wm, incs=[("sq", 1)])
                for mf in range(NCH):
                    b = mf % 2
                    pp = pDn[b]
                    wx = [("sq", P.n("sq"))] if mf == 0 else []
                    wcd = wstream(b, [(wdn[ex, :, mf * 128:(mf + 1) * 128]
                                       .rearrange("(k p) c -> p k c", p=128), 0, 1024)],
                                  wait_extra=wx)
                    for k in range(8):
                        w0 = [wcd, ("fadd", max(0, P.n("fadd") - 1))] if k == 0 else []
                        if k == 0 and mf == 0:
                            w0.append(("sq", P.n("sq")))
                        if k == 0 and b == 0:
                            w0.append(("cwcp", P.n("cwcp")))
                        em("t", lambda e, b=b, k=k, pp=pp: e.matmul(
                            pp[:, 0:512], stb[b][:, k * 128:(k + 1) * 128],
                            ae[:, k * T: k * T + 512],
                            start=(k == 0), stop=(k == 7)), waits=w0)
                        em("t", lambda e, b=b, k=k, pp=pp: e.matmul(
                            pp[:, 512:1024], stb[b][:, k * 128:(k + 1) * 128],
                            ae[:, k * T + 512: k * T + 1024],
                            start=(k == 0), stop=(k == 7)),
                           incs=([("pT", 1)] if k == 7 else []))
                    em("v", lambda e, mf=mf, pp=pp: e.scalar_tensor_tensor(
                        facc_t[:, mf * T: mf * T + 1024], pp[:], 1.0,
                        facc_t[:, mf * T: mf * T + 1024], ALU.mult, ALU.add),
                       waits=[("pT", P.n("pT"))], incs=[("fadd", 1)])
            P.wait("sy", "dmab0")
            P.wait("sy", "dmab1")
            P.play(name)
            moS.close()

        name_scr = {}

        ln_phase(lambda ti: x_in[ti * 128:(ti + 1) * 128, :], None, True, False,
                 None, "ln1")

        with ExitStack() as ffn1:
            facc = SB(ffn1, "facc1", [128, NCH * T], F32)
            hab = SB(ffn1, "hab1", [128, 16384], BF16)
            name_scr["hrl"] = SB(ffn1, "hrl1", [128, 1024], BF16)
            name_scr["hrl2"] = SB(ffn1, "hrlb1", [128, 1024], BF16)
            name_scr["cwb"] = SB(ffn1, "cwb1", [128, 1024], BF16)
            name_scr["eyeexp"] = SB(ffn1, "eye1", [16, E * 128], F32)
            em("sy", lambda e: e.dma_start(name_scr["eyeexp"][:], c_eyeexp),
               incs=[("dey", 16)])
            P.wait("t", "dey")
            em("v", lambda e: e.memset(facc[:], 0.0), incs=[("fz", 1)])
            P.wait("t", "fz")
            P.play("fz1")
            shared_phase(facc, hab, "sh1")
            moe_phase(facc, hab, "moe1")
            ln_phase(lambda ti: x_in[ti * 128:(ti + 1) * 128, :], xbuf, False, True,
                     facc, "upd1")

        rg8 = [[0, 1, 2, 3, 4, 5, 6, 7]]
        with ExitStack() as attnO:
            ctxn = SB(attnO, "ctxn", [128, 16384], BF16)
            qbrow = SB(attnO, "qbrow", [1, 128], F32)
            kvS2 = ExitStack()
            Kloc = SB(kvS2, "Kloc", [128, 4096], BF16)
            Vloc = SB(kvS2, "Vloc", [128, 4160], BF16)
            with ExitStack() as qkvS:
                qC = PS(qkvS, "qC", [128, 512], F32)
                qD = PS(qkvS, "qD", [128, 512], F32)
                qE = PS(qkvS, "qE", [128, 512], F32)
                qF = PS(qkvS, "qF", [128, 512], F32)
                wvtb = SB(qkvS, "wvtb", [128, 8192], BF16)
                wktb = SB(qkvS, "wktb", [128, 8192], BF16)
                qstg = [SB(qkvS, "qstg%d" % i, [128, 1024], BF16) for i in range(2)]
                sqb = SB(qkvS, "sqb", [128, 1024], BF16)
                for j in range(4):
                    wstream(j % 2, [(wvt[j * 512:(j + 1) * 512, :]
                                     .rearrange("(k p) n -> p k n", p=128), j * 2048, 2048)],
                            dst=wvtb)
                for j in range(4):
                    wstream(j % 2, [(wkt[j * 512:(j + 1) * 512, :]
                                     .rearrange("(k p) n -> p k n", p=128), j * 2048, 2048)],
                            dst=wktb)
                wk_done = [("sc0", P.n("sc0")), ("sc1", P.n("sc1"))]
                em("v", lambda e: e.memset(Vloc[:], 1.0))
                for t_ in range(NT):
                    for k in range(NCH):
                        w0 = (wk_done + [("vev", P.n("vev"))]) if k == 0 else []
                        em("t", lambda e, t_=t_, k=k: e.matmul(
                            qD[:], yfm[:, k * T + t_ * 128: k * T + (t_ + 1) * 128],
                            wvtb[:, k * 512:(k + 1) * 512],
                            start=(k == 0), stop=(k == NCH - 1)),
                           waits=w0, incs=([("pT", 1)] if k == NCH - 1 else []))
                    em("v", lambda e, t_=t_: e.tensor_copy(
                        Vloc[:, t_ * 520:(t_ + 1) * 520]
                        .rearrange("p (kv c) -> p kv c", c=65)[:, :, 0:64],
                        qD[:].rearrange("p (kv c) -> p kv c", c=64)),
                       waits=[("pT", P.n("pT"))], incs=[("vev", 1)])
                for Pp in range(4):
                    for n_ in range(2):
                        for k in range(NCH):
                            w0 = [("kev", P.n("kev")), ("vev", P.n("vev"))] if k == 0 else []
                            em("t", lambda e, Pp=Pp, n_=n_, k=k: e.matmul(
                                qC[:], wktb[:, k * 512 + Pp * 128: k * 512 + (Pp + 1) * 128],
                                yfm[:, k * T + n_ * 512: k * T + n_ * 512 + 512],
                                start=(k == 0), stop=(k == NCH - 1)),
                               waits=w0, incs=([("pT", 1)] if k == NCH - 1 else []))
                        em("v", lambda e, Pp=Pp, n_=n_: e.tensor_copy(
                            Kloc[:, Pp * 1024 + n_ * 512: Pp * 1024 + n_ * 512 + 512], qC[:]),
                           waits=[("pT", P.n("pT"))], incs=[("kev", 1)])
                for n_ in range(2):
                    pk = qF if n_ == 0 else qE
                    for Pp in range(4):
                        em("s", lambda e, Pp=Pp, n_=n_: e.activation(
                            sqb[:, 0:512],
                            Kloc[:, Pp * 1024 + n_ * 512: Pp * 1024 + n_ * 512 + 512],
                            AF.Square),
                           waits=[("kev", P.n("kev")), ("knm", P.n("knm"))],
                           incs=[("ksq", 1)])
                        em("t", lambda e, pk=pk, Pp=Pp: e.matmul(
                            pk[0:1, 0:512], onesb[:, 0:1], sqb[:, 0:512],
                            start=(Pp == 0), stop=(Pp == 3)),
                           waits=[("ksq", P.n("ksq"))], incs=[("knm", 1)])
                em("v", lambda e: e.tensor_reduce(col[0:1, 8:9], qF[0:1, 0:512], AX.X, ALU.max),
                   waits=[("knm", P.n("knm"))])
                em("v", lambda e: e.tensor_reduce(col[0:1, 9:10], qE[0:1, 0:512], AX.X, ALU.max))
                em("v", lambda e: e.tensor_max(col[0:1, 10:11], col[0:1, 8:9], col[0:1, 9:10]),
                   incs=[("knr", 1)])
                em("s", lambda e: e.activation(col[0:1, 11:12], col[0:1, 10:11], AF.Sqrt),
                   waits=[("knr", P.n("knr"))], incs=[("kns", 1)])
                em("v", lambda e: e.memset(tk[0:1, 0:8], 0.0))
                em("v", lambda e: e.tensor_copy(tk[0:1, 0:1], col[0:1, 11:12]),
                   waits=[("kns", P.n("kns"))], incs=[("kn8", 1)])
                for Pp in range(4):
                    for h in range(4):
                        idx = Pp * 4 + h
                        b = idx % 2
                        wcd = wstream(b, [(wqt[:, idx * 128:(idx + 1) * 128]
                                           .rearrange("(k p) c -> p k c", p=128), 0, 2048)])
                        for n_ in range(2):
                            for k in range(NCH):
                                w0 = [wcd, ("qev", P.n("qev")),
                                      ("knm", P.n("knm"))] if k == 0 else []
                                em("t", lambda e, b=b, k=k, n_=n_: e.matmul(
                                    qC[:], stb[b][:, k * 128:(k + 1) * 128],
                                    yfm[:, k * T + n_ * 512: k * T + n_ * 512 + 512],
                                    start=(k == 0), stop=(k == NCH - 1)),
                                   waits=w0, incs=([("pT", 1)] if k == NCH - 1 else []))
                            em("v", lambda e, idx=idx, n_=n_: e.tensor_copy(
                                qstg[idx % 2][:, n_ * 512:(n_ + 1) * 512], qC[:]),
                               waits=[("pT", P.n("pT")),
                                      ("dq%d" % (idx % 2), P.n("dq%d" % (idx % 2)))],
                               incs=[("qev", 1)])
                        em("s", lambda e, idx=idx: e.activation(
                            sqb[:], qstg[idx % 2][:], AF.Square),
                           waits=[("qev", P.n("qev")), ("qnm", P.n("qnm"))],
                           incs=[("qsq", 1)])
                        for n_ in range(2):
                            pk = qF if n_ == 0 else qE
                            em("t", lambda e, pk=pk, n_=n_: e.matmul(
                                pk[0:1, 0:512], onesb[:, 0:1],
                                sqb[:, n_ * 512:(n_ + 1) * 512],
                                start=True, stop=True),
                               waits=[("qsq", P.n("qsq")), ("qnr", P.n("qnr"))],
                               incs=[("qnm", 1)])
                            em("v", lambda e, pk=pk, idx=idx, n_=n_: e.tensor_reduce(
                                qbrow[0:1, idx * 8 + n_ * 4: idx * 8 + n_ * 4 + 4],
                                pk[0:1, 0:512].rearrange("p (s j) -> p s j", j=128),
                                AX.X, ALU.max),
                               waits=[("qnm", P.n("qnm"))], incs=[("qnr", 1)])
                        em("sy", lambda e, Pp=Pp, h=h, idx=idx: e.dma_start(
                            qdr[Pp, :, h * 1024:(h + 1) * 1024], qstg[idx % 2][:]),
                           waits=[("qev", P.n("qev")), ("qsq", P.n("qsq"))],
                           incs=[("dq%d" % (idx % 2), 16)])
                em("sy", lambda e: e.dma_start(
                    kag_i.rearrange("(Pp p) t -> p Pp t", p=128),
                    Kloc[:].rearrange("p (Pp t) -> p Pp t", t=1024)),
                   waits=[("kev", P.n("kev"))], incs=[("dag", 16)])
                em("sy", lambda e: e.dma_start(
                    vag_i.rearrange("(s p) c -> p s c", p=128),
                    Vloc[:].rearrange("p (s c) -> p s c", c=520)),
                   waits=[("vev", P.n("vev"))], incs=[("dag", 16)])
                em("sy", lambda e: e.dma_start(nrm_i, tk[0:1, 0:8]),
                   waits=[("kn8", P.n("kn8"))], incs=[("dag", 16)])
                P.wait("g", "dag")
                em("g", lambda e: e.collective_compute(
                    "AllGather", ALU.bypass, rg8, [kag_i.opt()], [kag_o.opt()]),
                   incs=[("ccs", 1)], dma=True)
                em("g", lambda e: e.collective_compute(
                    "AllGather", ALU.bypass, rg8, [vag_i.opt()], [vag_o.opt()]),
                   incs=[("ccs", 1)], dma=True)
                em("g", lambda e: e.collective_compute(
                    "AllGather", ALU.bypass, rg8, [nrm_i.opt()], [nrm_o.opt()]),
                   incs=[("ccs", 1)], dma=True)
                P.wait("sy", "dq0")
                P.wait("sy", "dq1")
                P.wait("sy", "dmab0")
                P.wait("sy", "dmab1")
                P.play("qkv")

            with ExitStack() as passS:
                Kp = SB(passS, "Kp", [128, 8192], BF16)
                Vp = SB(passS, "Vp", [128, 64 * 130], BF16)
                Qp = SB(passS, "Qp", [128, 4096], BF16)
                atb = SB(passS, "atb", [128, 4 * 512], BF16)
                escr = SB(passS, "escr", [128, 2304], F32)
                recb = SB(passS, "recb", [64, 512], F32)
                ostg = SB(passS, "ostg", [64, 512], BF16)
                BT = SB(passS, "BT", [128, 2080], BF16)
                sS = [PS(passS, "sS%d" % i, [128, 512], F32) for i in range(4)]
                cC = PS(passS, "cC", [128, 512], F32)
                cD = PS(passS, "cD", [128, 512], F32)
                bX = PS(passS, "bX", [128, 512], F32)
                BTr = escr
                P.wait("v", "ccs")
                QM = 2096
                QP = 2224
                CR = 2256
                KM = 2088
                em("sy", lambda e: e.dma_start(
                    escr[0:1, KM:KM + 8],
                    nrm_o[:, 0:1].rearrange("r c -> c r"), allow_slow_non_contiguous=True),
                   waits=[("ccs", P.n("ccs"))], incs=[("dbt", 16)])
                em("v", lambda e: e.tensor_reduce(col[0:1, 12:13], escr[0:1, KM:KM + 8],
                                                  AX.X, ALU.max),
                   waits=[("dbt", P.n("dbt"))])
                em("s", lambda e: e.activation(escr[0:1, QM:QM + 128], qbrow[:], AF.Sqrt),
                   waits=[("qnr", P.n("qnr"))], incs=[("bts", 1)])
                for Pp in range(4):
                    for s_ in range(8):
                        em("v", lambda e, Pp=Pp, s_=s_: e.tensor_reduce(
                            escr[0:1, QP + Pp * 8 + s_: QP + Pp * 8 + s_ + 1],
                            bass.AP(escr, QM + Pp * 32 + s_, [[2304, 1], [8, 4]]),
                            AX.X, ALU.max),
                           waits=([("bts", P.n("bts"))] if (Pp == 0 and s_ == 0) else []))
                em("v", lambda e: e.tensor_single_scalar(
                    escr[0:1, CR:CR + 32], escr[0:1, QP:QP + 32], col[0:1, 12:13],
                    ALU.mult))
                em("v", lambda e: e.tensor_scalar_mul(escr[0:1, CR:CR + 32],
                                                      escr[0:1, CR:CR + 32], -0.125),
                   incs=[("btc", 1)])
                for Pp in range(4):
                    for s_ in range(8):
                        em("v", lambda e, Pp=Pp, s_=s_: e.tensor_single_scalar(
                            BTr[0:1, (Pp * 8 + s_) * 65: (Pp * 8 + s_ + 1) * 65],
                            pads[0:1, s_ * 65:(s_ + 1) * 65],
                            escr[0:1, CR + Pp * 8 + s_: CR + 1 + Pp * 8 + s_], ALU.add),
                           incs=([("btr", 1)] if (Pp == 3 and s_ == 7) else []))
                for j in range(5):
                    w = min(512, 2080 - j * 512)
                    em("t", lambda e, j=j, w=w: e.matmul(
                        bX[:, 0:w], onesf[0:1, 0:128], BTr[0:1, j * 512: j * 512 + w],
                        start=True, stop=True),
                       waits=[("btr", P.n("btr")), ("btv", P.n("btv"))], incs=[("btm", 1)])
                    em("v", lambda e, j=j, w=w: e.tensor_copy(
                        BT[:, j * 512: j * 512 + w], bX[:, 0:w]),
                       waits=[("btm", P.n("btm"))], incs=[("btv", 1)])
                for Pp in range(4):
                    wP = [("avd", P.n("avd")), ("epv", P.n("epv"))]
                    em("sy", lambda e, Pp=Pp: e.dma_start(Qp[:], qdr[Pp, :, :]),
                       waits=wP + [("dkp", P.n("dkp"))], incs=[("dkp", 16)])
                    KpV = Kp[:].rearrange("p (s c j) -> p c s j", s=8, c=8, j=128)
                    KAGv = kag_o.rearrange("(c Pq p) (s j) -> c Pq p s j",
                                           c=8, Pq=4, j=128)
                    for cc in range(8):
                        em("sy", lambda e, cc=cc, Pp=Pp, KpV=KpV, KAGv=KAGv: e.dma_start(
                            KpV[:, cc:cc + 1, 0:4, :],
                            KAGv[cc:cc + 1, Pp:Pp + 1, :, 0:4, :]
                            .rearrange("c Pq p s j -> p (c Pq) s j")),
                           waits=([("dkp", P.n("dkp"))] if cc == 0 else []),
                           incs=[("dkp", 16)])
                        em("sy", lambda e, cc=cc, Pp=Pp, KpV=KpV, KAGv=KAGv: e.dma_start(
                            KpV[:, 7 - cc:8 - cc, 4:8, :],
                            KAGv[cc:cc + 1, Pp:Pp + 1, :, 4:8, :]
                            .rearrange("c Pq p s j -> p (c Pq) s j")),
                           incs=[("dkp", 16)])
                    VpV = Vp[:].rearrange("p (s c j) -> p c s j", s=8, c=8, j=130)
                    VAGv = vag_o.rearrange("(c s p) q -> c s p q", c=8, p=128)
                    for cc in range(8):
                        em("sy", lambda e, cc=cc, Pp=Pp, VpV=VpV, VAGv=VAGv: e.dma_start(
                            VpV[:, cc:cc + 1, 0:4, :],
                            VAGv[cc:cc + 1, 0:4, :, 130 * Pp:130 * Pp + 130]
                            .transpose([2, 0, 1, 3])),
                           waits=([("dvp", P.n("dvp"))] if cc == 0 else []),
                           incs=[("dvp", 16)])
                        em("sy", lambda e, cc=cc, Pp=Pp, VpV=VpV, VAGv=VAGv: e.dma_start(
                            VpV[:, 7 - cc:8 - cc, 4:8, :],
                            VAGv[cc:cc + 1, 4:8, :, 130 * Pp:130 * Pp + 130]
                            .transpose([2, 0, 1, 3])),
                           incs=[("dvp", 16)])
                    ld_done = [("dkp", P.n("dkp")), ("dvp", P.n("dvp")),
                               ("btv", P.n("btv")), ("dbt", P.n("dbt"))]
                    for s_ in range(8):
                        nfull = LF[s_]
                        qv = Qp[:].rearrange("p (h s j) -> p h s j", h=4, j=128)
                        pts_after = {}

                        def emit_S(i, first, Pp=Pp, s_=s_, nfull=nfull, qv=qv,
                                   ld_done=ld_done, pts_after=pts_after):
                            diag = (i == nfull)
                            SA = sS[2 * (i % 2)]
                            SBp = sS[2 * (i % 2) + 1]
                            if diag:
                                lhsE = Kloc[0:64, Pp * 1024 + s_ * 128: Pp * 1024 + (s_ + 1) * 128]
                                lhsO = Kloc[64:128, Pp * 1024 + s_ * 128: Pp * 1024 + (s_ + 1) * 128]
                            else:
                                lhsE = Kp[0:64, i * 128:(i + 1) * 128]
                                lhsO = Kp[64:128, i * 128:(i + 1) * 128]
                            rhsE = qv[0:64, :, s_:s_ + 1, :]
                            rhsO = qv[64:128, :, s_:s_ + 1, :]
                            w0 = [("exd", P.n("exd"))]
                            if first:
                                w0 += ld_done + [("epv", P.n("epv"))]
                            em("t", lambda e, SA=SA, lhsE=lhsE, rhsE=rhsE: e.matmul(
                                SA[:, 0:512], lhsE, rhsE, start=True, stop=True),
                               waits=w0)
                            em("t", lambda e, SBp=SBp, lhsO=lhsO, rhsO=rhsO: e.matmul(
                                SBp[:, 0:512], lhsO, rhsO, start=True, stop=True),
                               incs=[("pTs", 1)])
                            pts_after[i] = P.n("pTs")
                            if diag:
                                em("v", lambda e, SA=SA: e.tensor_add(
                                    SA[:, 0:512], SA[:, 0:512], trils[:]),
                                   waits=[("pTs", P.n("pTs"))])
                                em("v", lambda e, SBp=SBp: e.tensor_add(
                                    SBp[:, 0:512], SBp[:, 0:512], trils[:]),
                                   incs=[("tri", 1)])

                        emit_S(0, True)
                        for i in range(nfull + 1):
                            diag = (i == nfull)
                            if not diag:
                                emit_S(i + 1, False)
                            SA = sS[2 * (i % 2)]
                            SBp = sS[2 * (i % 2) + 1]
                            bidx = (Pp * 8 + s_) * 65 + (64 if diag else i)
                            a0 = atb[:, (i % 2) * 1024: (i % 2) * 1024 + 512]
                            a1_ = atb[:, (i % 2) * 1024 + 512: (i % 2) * 1024 + 1024]
                            we = [("pTs", pts_after[i]), ("avd", max(0, P.n("avd") - 2))]
                            if diag:
                                we.append(("tri", P.n("tri")))
                            em("s", lambda e, SA=SA, a0=a0, bidx=bidx: e.activation(
                                a0, SA[:, 0:512], AF.Exp, bias=BT[:, bidx:bidx + 1],
                                scale=0.125),
                               waits=we, incs=[("exd", 1)])
                            em("s", lambda e, SBp=SBp, a1_=a1_, bidx=bidx: e.activation(
                                a1_, SBp[:, 0:512], AF.Exp, bias=BT[:, bidx:bidx + 1],
                                scale=0.125),
                               incs=[("exd", 1)])
                            if diag:
                                vE = Vloc[:, s_ * 520: s_ * 520 + 520]
                                lv0 = vE.rearrange("p (kv c) -> p kv c", c=65)[:, 2 * Pp:2 * Pp + 1, :]
                                lv1 = vE.rearrange("p (kv c) -> p kv c", c=65)[:, 2 * Pp + 1:2 * Pp + 2, :]
                            else:
                                lv0 = Vp[:, i * 130: i * 130 + 65]
                                lv1 = Vp[:, i * 130 + 65: i * 130 + 130]
                            em("t", lambda e, lv0=lv0, a0=a0, i=i, diag=diag: e.matmul(
                                cC[0:65, :], lv0, a0, start=(i == 0), stop=diag),
                               waits=[("exd", P.n("exd"))], incs=[("avd", 1)])
                            em("t", lambda e, lv1=lv1, a1_=a1_, i=i, diag=diag: e.matmul(
                                cD[0:65, :], lv1, a1_, start=(i == 0), stop=diag),
                               incs=[("avd", 1)])
                        for par in range(2):
                            ctx = cC if par == 0 else cD
                            bc = bX
                            em("s", lambda e, ctx=ctx, par=par: e.activation(
                                escr[64:65, 0:512], ctx[64:65, 0:512], AF.Ln),
                               waits=[("avd", P.n("avd")), ("ebm", P.n("ebm"))],
                               incs=[("eln", 1)])
                            em("s", lambda e: e.activation(
                                escr[64:65, 512:1024], escr[64:65, 0:512], AF.Exp,
                                scale=-1.0),
                               incs=[("erc", 1)])
                            em("t", lambda e, bc=bc: e.matmul(
                                bc[0:64, 0:512], onesf[64:65, 0:64],
                                escr[64:65, 512:1024], start=True, stop=True),
                               waits=[("erc", P.n("erc")), ("ebv", P.n("ebv"))],
                               incs=[("ebm", 1)])
                            em("v", lambda e, bc=bc: e.tensor_copy(recb[:], bc[0:64, 0:512]),
                               waits=[("ebm", P.n("ebm"))], incs=[("ebv", 1)])
                            off = s_ * 2048 + Pp * 512
                            if par == 0:
                                em("v", lambda e, ctx=ctx, off=off: e.tensor_tensor(
                                    ctxn[0:64, off:off + 512], ctx[0:64, 0:512],
                                    recb[:], ALU.mult),
                                   waits=[("ebv", P.n("ebv"))], incs=[("epv", 1)])
                            else:
                                em("v", lambda e, ctx=ctx: e.tensor_tensor(
                                    ostg[:], ctx[0:64, 0:512], recb[:], ALU.mult),
                                   waits=[("ebv", P.n("ebv")),
                                          ("dcx", P.n("dcx"))], incs=[("epv", 1)])
                                em("g", lambda e, off=off: e.dma_start(
                                    ctxn[64:128, off:off + 512], ostg[:]),
                                   waits=[("epv", P.n("epv"))], incs=[("dcx", 16)],
                                   dma=True)
                P.wait("g", "dcx")
                P.wait("sy", "dkp")
                P.wait("sy", "dvp")
                P.play("attn")

            kvS2.close()
            with ExitStack() as woS:
                wotb = SB(woS, "wotb", [128, 16 * 2048], BF16)
                pW0 = PS(woS, "pW0", [128, 1024], F32)
                pW1 = PS(woS, "pW1", [128, 1024], F32)
                wga = PS(woS, "wga", [128, 1024], BF16)
                wgb = PS(woS, "wgb", [128, 1024], BF16)
                wlp = {"ga": wga, "gb": wgb, "tf": pW0, "lg": pW1}
                pso = [pW0[:, 0:512], pW0[:, 512:1024],
                       pW1[:, 0:512], pW1[:, 512:1024]]
                for c in range(NCH):
                    wstream(c % 2, [(wot[c * 128:(c + 1) * 128, :]
                                     .rearrange("p (k c2) -> p k c2", c2=128),
                                     c * 2048, 2048)], dst=wotb)
                wo_done = [("sc0", P.n("sc0")), ("sc1", P.n("sc1")),
                           ("dcx", P.n("dcx")), ("epv", P.n("epv"))]
                for s_ in range(8):
                    em("sy", lambda e, s_=s_: e.dma_start(
                        xt[:], xbuf[s_ * 128:(s_ + 1) * 128, :]),
                       waits=[("pV", P.n("pV")), ("pS", P.n("pS")),
                              ("dxt", P.n("dxt"))],
                       incs=[("dxt", 16)])
                    for j in range(4):
                        for c in range(NCH):
                            w0 = (wo_done + [("pV", P.n("pV")), ("cwe", P.n("cwe")),
                                             ("a2r", P.n("a2r"))]) if c == 0 else []
                            em("t", lambda e, s_=s_, j=j, c=c: e.matmul(
                                pso[j], ctxn[:, s_ * 2048 + c * 128: s_ * 2048 + (c + 1) * 128],
                                wotb[:, c * 2048 + j * 512: c * 2048 + (j + 1) * 512],
                                start=(c == 0), stop=(c == NCH - 1)),
                               waits=w0, incs=([("pT", 1)] if c == NCH - 1 else []))
                    for j in range(4):
                        em("v", lambda e, j=j: e.scalar_tensor_tensor(
                            xt[:, j * 512:(j + 1) * 512], pso[j], 1.0,
                            xt[:, j * 512:(j + 1) * 512], ALU.mult, ALU.add),
                           waits=([("pT", P.n("pT")), ("dxt", P.n("dxt"))] if j == 0 else []),
                           incs=([("pV", 1)] if j == 3 else []))
                    emit_ln_core(s_, True, wlp)
                    em("sy", lambda e, s_=s_: e.dma_start(
                        xbuf[s_ * 128:(s_ + 1) * 128, :], xt[:]),
                       waits=[("pV", P.n("pV")), ("pS", P.n("pS"))], incs=[("dxt", 16)])
                P.wait("sy", "dxt")
                P.wait("sy", "dmab0")
                P.wait("sy", "dmab1")
                P.play("wo")

        with ExitStack() as ffn2:
            facc = SB(ffn2, "facc2", [128, NCH * T], F32)
            hab = SB(ffn2, "hab2", [128, 16384], BF16)
            name_scr["hrl"] = SB(ffn2, "hrl2", [128, 1024], BF16)
            name_scr["hrl2"] = SB(ffn2, "hrlb2", [128, 1024], BF16)
            name_scr["cwb"] = SB(ffn2, "cwb2", [128, 1024], BF16)
            name_scr["eyeexp"] = SB(ffn2, "eye2", [16, E * 128], F32)
            em("sy", lambda e: e.dma_start(name_scr["eyeexp"][:], c_eyeexp),
               incs=[("dey", 16)])
            P.wait("t", "dey")
            em("v", lambda e: e.memset(facc[:], 0.0))
            P.play("fz2")
            shared_phase(facc, hab, "sh2")
            moe_phase(facc, hab, "moe2")
            ln_phase(lambda ti: xbuf[ti * 128:(ti + 1) * 128, :], out_d, False, True,
                     facc, "fin", store_out_only=True)

    return nc



def _consts():
    eye = np.eye(128, dtype=np.float32)
    eyeexp = np.zeros((16, E * 128), dtype=np.float32)
    for e in range(E):
        eyeexp[e, e * 128:(e + 1) * 128] = 1.0
    kk = np.arange(128)[:, None]
    qq = np.arange(128)[None, :]
    blk = np.where(kk > qq, NEG, 0.0).astype(np.float32)
    return {
        "c_idbf": eye.astype(bf16), "c_idf": eye,
        "c_ones": np.ones((128, 128), np.float32),
        "c_iota16": np.tile(np.arange(16, dtype=np.float32), (128, 1)),
        "c_iota4": np.tile(np.arange(4, dtype=np.float32), (128, 1)),
        "c_i16m": np.tile(np.arange(16, dtype=np.float32) - 16.0, (128, 1)),
        "c_i4m": np.tile(np.arange(4, dtype=np.float32) - 4.0, (128, 1)),
        "c_onesb": np.ones((128, 1), bf16),
        "c_eyeexp": eyeexp,
        "c_tril": np.tile(blk, (1, 4)),
    }


def _pad_table(c):
    pad = np.zeros((1, 8 * 65), dtype=np.float32)
    for s in range(8):
        g = g_of(c, s)
        pad[0, s * 65 + g: s * 65 + 64] = NEG
    return pad


def _shard_x(x):
    x = np.asarray(x).reshape(8192, D)
    return [np.ascontiguousarray(
        np.concatenate([x[g * 128:(g + 1) * 128] for g in core_tiles(c)], axis=0))
        for c in range(8)]


def _perm_qorder():
    order = []
    for Pp in range(4):
        for h in range(4):
            order.extend(range((8 * Pp + h) * 64, (8 * Pp + h) * 64 + 64))
            order.extend(range((8 * Pp + 4 + h) * 64, (8 * Pp + 4 + h) * 64 + 64))
    return np.array(order)


def make_inputs(L0_attn_x, L0_attn_Wq, L0_attn_Wk, L0_attn_Wv, L0_attn_Wo,
                L0_ffn_router_weight, L0_ffn_e_bias, L0_ffn_w_up, L0_ffn_w_down,
                L0_ffn_shared_up, L0_ffn_shared_down):
    order = _perm_qorder()
    base = {
        "wqt": np.ascontiguousarray(np.asarray(L0_attn_Wq, np.float32).T[:, order]),
        "wkt": np.ascontiguousarray(np.asarray(L0_attn_Wk, np.float32).T),
        "wvt": np.ascontiguousarray(np.asarray(L0_attn_Wv, np.float32).T),
        "wot": np.ascontiguousarray(np.asarray(L0_attn_Wo, np.float32).T[order, :]),
        "rwt": np.ascontiguousarray(np.asarray(L0_ffn_router_weight, np.float32).T),
        "ebias": np.ascontiguousarray(np.asarray(L0_ffn_e_bias, np.float32).reshape(1, 16)),
        "wup": np.ascontiguousarray(np.asarray(L0_ffn_w_up, np.float32)),
        "wdn": np.ascontiguousarray(np.asarray(L0_ffn_w_down, np.float32)),
        "su": np.ascontiguousarray(np.asarray(L0_ffn_shared_up, np.float32)),
        "sd": np.ascontiguousarray(np.asarray(L0_ffn_shared_down, np.float32)),
    }
    base.update(_consts())
    xs = _shard_x(L0_attn_x)
    in_maps = []
    for c in range(8):
        m = dict(base)
        m["x0"] = xs[c]
        m["c_pad"] = _pad_table(c)
        in_maps.append(m)
    return in_maps


def kernel(**inputs):
    in_maps = make_inputs(**inputs)
    nc = build()
    res = run_bass_kernel_spmd(nc, in_maps, core_ids=list(range(8)))
    out = np.zeros((8192, D), dtype=np.float32)
    for c in range(8):
        oc = res.results[c]["out"]
        for s, g in enumerate(core_tiles(c)):
            out[g * 128:(g + 1) * 128] = oc[s * 128:(s + 1) * 128]
    return out.reshape(1, 8192, D)


# version 03
# speedup vs baseline: 1.3266x; 1.3266x over previous
from contextlib import ExitStack

import numpy as np
import ml_dtypes
import concourse.bass as bass
import concourse.mybir as mybir
from concourse.bass_utils import run_bass_kernel_spmd

F32 = mybir.dt.float32
BF16 = mybir.dt.bfloat16
AF = mybir.ActivationFunctionType
ALU = mybir.AluOpType
AX = mybir.AxisListType
bf16 = ml_dtypes.bfloat16

D = 2048
T = 1024
NT = 8
NCH = 16
E = 16
MI = 1024
SI = 4096
NEG = -30000.0
EPS = 1e-5

def core_tiles(c):
    return [c, 8 + c, 16 + c, 24 + c, 39 - c, 47 - c, 55 - c, 63 - c]

def g_of(c, s):
    return 8 * s + c if s < 4 else 8 * s + 7 - c

LF = [8 * s + 7 for s in range(8)]


class Prog:

    KEYS = ("sy", "t", "v", "s", "g")
    CHAINED = ("v", "s", "g")

    def __init__(self, nc, stack):
        self.nc = nc
        self.stack = stack
        self.q = {k: [] for k in self.KEYS}
        self.sems = {}
        self.cnt = {k: 0 for k in self.CHAINED}
        self.virt = {}
        for k in self.CHAINED:
            self.sem("self_" + k)

    def sem(self, name):
        if name not in self.sems:
            h = self.stack.enter_context(self.nc.semaphore(name))
            self.sems[name] = [h, 0]
        return name

    def n(self, name):
        if name in self.virt:
            return len(self.virt[name][1])
        return self.sems[self.sem(name)][1]

    def _xlate(self, nm, v):
        v = int(v)
        if nm in self.virt:
            eng, lst = self.virt[nm]
            return ("self_" + eng, 0 if v <= 0 else lst[min(v, len(lst)) - 1])
        return (self.sem(nm), v)

    def emit(self, eng, build, waits=(), incs=(), dma=False):
        ws = [self._xlate(nm, v) for nm, v in waits]
        real = []
        pos = None
        chained = (eng in self.CHAINED) and not dma
        if build is not None and chained:
            self.cnt[eng] += 1
            pos = self.cnt[eng]
        for nm, k in incs:
            if chained:
                assert build is not None and k == 1, (eng, nm, k)
                ent = self.virt.setdefault(nm, (eng, []))
                assert ent[0] == eng, (nm, ent[0], eng)
                ent[1].append(pos)
            else:
                assert nm not in self.virt, nm
                self.sem(nm)
                self.sems[nm][1] += k
                real.append((nm, k))
        assert len(real) <= 1, (eng, incs)
        self.q[eng].append((build, ws, real, pos))

    def wait(self, eng, name, val=None):
        self.emit(eng, None, waits=[(name, self.n(name) if val is None else val)])

    def play(self, name):
        attr = {"sy": "sync", "t": "tensor", "v": "vector", "s": "scalar", "g": "gpsimd"}
        with self.nc.Block(name) as blk:
            for key in self.KEYS:
                ops = self.q[key]
                if not ops:
                    continue
                chain = key in self.CHAINED
                snm = ("self_" + key) if chain else None

                def body(eng, ops=ops, chain=chain, snm=snm):
                    for build, ws, real, pos in ops:
                        for nm, v in ws:
                            eng.wait_ge(self.sems[nm][0], v)
                        if build is None:
                            continue
                        if chain and pos is not None:
                            eng.wait_ge(self.sems[snm][0], pos - 1)
                        inst = build(eng)
                        if pos is not None:
                            inst.then_inc(self.sems[snm][0], 1)
                            self.sems[snm][1] += 1
                        else:
                            for nm, k in real:
                                inst.then_inc(self.sems[nm][0], k)

                getattr(blk, attr[key])(body)
        self.q = {k: [] for k in self.KEYS}


def build(debug=False):
    nc = bass.Bass("TRN2", target_bir_lowering=False)
    dt = nc.dram_tensor

    x_in = dt("x0", [T, D], F32, kind="ExternalInput").ap()
    wqt = dt("wqt", [D, D], F32, kind="ExternalInput").ap()
    wkt = dt("wkt", [D, 512], F32, kind="ExternalInput").ap()
    wvt = dt("wvt", [D, 512], F32, kind="ExternalInput").ap()
    wot = dt("wot", [D, D], F32, kind="ExternalInput").ap()
    rwt = dt("rwt", [D, 16], F32, kind="ExternalInput").ap()
    ebias_d = dt("ebias", [1, 16], F32, kind="ExternalInput").ap()
    wup = dt("wup", [E, D, MI], F32, kind="ExternalInput").ap()
    wdn = dt("wdn", [E, MI, D], F32, kind="ExternalInput").ap()
    su_d = dt("su", [D, SI], F32, kind="ExternalInput").ap()
    sd_d = dt("sd", [SI, D], F32, kind="ExternalInput").ap()
    c_idbf = dt("c_idbf", [128, 128], BF16, kind="ExternalInput").ap()
    c_idf = dt("c_idf", [128, 128], F32, kind="ExternalInput").ap()
    c_ones = dt("c_ones", [128, 128], F32, kind="ExternalInput").ap()
    c_iota16 = dt("c_iota16", [128, 16], F32, kind="ExternalInput").ap()
    c_iota4 = dt("c_iota4", [128, 4], F32, kind="ExternalInput").ap()
    c_i16m = dt("c_i16m", [128, 16], F32, kind="ExternalInput").ap()
    c_i4m = dt("c_i4m", [128, 4], F32, kind="ExternalInput").ap()
    c_onesb = dt("c_onesb", [128, 1], BF16, kind="ExternalInput").ap()
    c_onesbr = dt("c_onesbr", [1, 128], BF16, kind="ExternalInput").ap()
    c_eyeexp = dt("c_eyeexp", [16, E * 128], BF16, kind="ExternalInput").ap()
    c_tril = dt("c_tril", [128, 512], BF16, kind="ExternalInput").ap()
    c_pad = dt("c_pad", [1, 8 * 65], BF16, kind="ExternalInput").ap()

    out_d = dt("out", [T, D], F32, kind="ExternalOutput").ap()

    xbuf = dt("xbuf", [T, D], F32).ap()
    qdr = dt("qdr", [4, 128, 4096], BF16).ap()
    kag_i = dt("kag_i", [512, 1024], BF16).ap()
    kag_o = dt("kag_o", [4096, 1024], BF16).ap()
    vag_i = dt("vag_i", [1024, 520], BF16).ap()
    vag_o = dt("vag_o", [8192, 520], BF16).ap()
    nrm_i = dt("nrm_i", [1, 8], F32).ap()
    nrm_o = dt("nrm_o", [8, 8], F32).ap()

    root = ExitStack()

    def SB(st, name, shape, d):
        return st.enter_context(nc.sbuf_tensor(name, shape, d))

    def PS(st, name, shape, d):
        return st.enter_context(nc.psum_tensor(name, shape, d))

    with root:
        yfm = SB(root, "yfm", [128, NCH * T], BF16)
        idbf = SB(root, "idbf", [128, 128], BF16)
        idf = SB(root, "idf", [128, 128], F32)
        onesf = SB(root, "onesf", [128, 128], F32)
        onesb = SB(root, "onesb", [128, 1], BF16)
        onesbr = SB(root, "onesbr", [1, 128], BF16)
        iota16 = SB(root, "iota16", [128, 16], F32)
        iota4 = SB(root, "iota4", [128, 4], F32)
        i16m = SB(root, "i16m", [128, 16], F32)
        i4m = SB(root, "i4m", [128, 4], F32)
        rwts = SB(root, "rwts", [128, 16 * 16], F32)
        ebb = SB(root, "ebb", [128, 16], F32)
        ebrow = SB(root, "ebrow", [1, 16], F32)
        cwT = SB(root, "cwT", [16, T], BF16)
        rsc = SB(root, "rsc", [128, 2 * 128], F32)
        col = SB(root, "col", [128, 16], F32)
        tk = SB(root, "tk", [128, 12 * 16], F32)
        tk4 = SB(root, "tk4", [128, 16 * 4], F32)
        c4 = SB(root, "c4", [128, 4], F32)
        c16s = SB(root, "c16s", [128, 16], F32)
        neg30 = SB(root, "neg30", [128, 16], F32)
        epscol = SB(root, "epscol", [128, 1], F32)
        stf = [SB(root, "stf%d" % i, [128, 4096], F32) for i in range(2)]
        stb = [SB(root, "stb%d" % i, [128, 4096], BF16) for i in range(2)]


        P = Prog(nc, root)
        em = P.emit

        def wstream(b, parts, wait_extra=(), dst=None, guard=True):
            scb = "sc%d" % b
            dmb = "dmab%d" % b
            dstt = stb[b] if dst is None else dst
            for j, (srcap, off, w) in enumerate(parts):
                em("sy", lambda e, srcap=srcap, w=w, b=b: e.dma_start(
                    stf[b][:, 0:w].rearrange("p (k c) -> p k c", c=128), srcap),
                   waits=[(scb, P.n(scb))] + (list(wait_extra) if j == 0 else []),
                   incs=[(dmb, 16)])
                ws = [(dmb, P.n(dmb))]
                if j == 0 and guard:
                    ws.append(("pT", max(0, P.n("pT") - 1)))
                em("v", lambda e, off=off, w=w, dstt=dstt, b=b: e.tensor_copy(
                    dstt[:, off:off + w], stf[b][:, 0:w]),
                   waits=ws, incs=[(scb, 1)])
            return (scb, P.n(scb))

        for dst, src in [(idbf, c_idbf), (idf, c_idf), (onesf, c_ones),
                         (iota16, c_iota16), (iota4, c_iota4),
                         (i16m, c_i16m), (i4m, c_i4m), (onesb, c_onesb),
                         (onesbr, c_onesbr),
                         (ebrow, ebias_d)]:
            em("sy", lambda e, a=dst, b=src: e.dma_start(a[:], b), incs=[("dcn", 16)])
        em("sy", lambda e: e.dma_start(rwts[:].rearrange("p (k e) -> p k e", e=16),
                                       rwt.rearrange("(k p) e -> p k e", p=128)),
           incs=[("dcn", 16)])
        em("v", lambda e: e.memset(neg30[:], -1e30))
        em("v", lambda e: e.memset(epscol[:], EPS))
        em("v", lambda e: e.memset(c4[:], 4.0))
        em("v", lambda e: e.memset(c16s[:], 16.0))
        with ExitStack() as cS:
            pcn = PS(cS, "pcn", [128, 512], F32)
            em("t", lambda e, pcn=pcn: e.matmul(pcn[:, 0:16], onesf[0:1, 0:128],
                                                ebrow[0:1, 0:16], start=True, stop=True),
               waits=[("dcn", P.n("dcn"))], incs=[("pT", 1)])
            em("v", lambda e, pcn=pcn: e.tensor_copy(ebb[:], pcn[:, 0:16]),
               waits=[("pT", P.n("pT"))])
            P.wait("sy", "dcn")
            P.play("consts")

        def topk_emit(ti, lp):
            S_ = lambda i: tk[:, i * 16:(i + 1) * 16]
            G_ = lambda i: tk4[:, i * 4:(i + 1) * 4]
            C_ = lambda i: col[:, 8 + i:9 + i]
            lg, tf = lp["lg"], lp["tf"]
            em("s", lambda e, lg=lg: e.activation(S_(0), lg[:, 0:16], AF.Sigmoid),
               waits=[("pT", P.n("pT"))], incs=[("pS", 1)])
            em("v", lambda e: e.tensor_add(S_(1), S_(0), ebb[:]),
               waits=[("pS", P.n("pS"))])
            sv = lambda j: S_(1).rearrange("p (g j) -> p g j", j=4)[:, :, j:j + 1]
            em("v", lambda e: e.tensor_max(G_(0), sv(0), sv(1)))
            em("v", lambda e: e.tensor_max(G_(1), sv(2), sv(3)))
            em("v", lambda e: e.tensor_add(G_(2), sv(0), sv(1)))
            em("v", lambda e: e.tensor_add(G_(3), sv(2), sv(3)))
            em("v", lambda e: e.tensor_add(G_(0), G_(0), G_(1)))
            em("v", lambda e: e.tensor_max(G_(0), G_(0), G_(2)))
            em("v", lambda e: e.tensor_max(G_(0), G_(0), G_(3)))
            em("v", lambda e: e.tensor_reduce(C_(0), G_(0), AX.X, ALU.max))
            em("v", lambda e: e.tensor_single_scalar(G_(1), G_(0), C_(0), ALU.is_ge))
            em("v", lambda e: e.tensor_mul(G_(2), G_(1), i4m[:]))
            em("v", lambda e: e.tensor_add(G_(2), G_(2), c4[:]))
            em("v", lambda e: e.tensor_reduce(C_(1), G_(2), AX.X, ALU.min))
            em("v", lambda e: e.tensor_single_scalar(G_(1), iota4[:], C_(1), ALU.is_equal))
            em("v", lambda e: e.tensor_mul(G_(3), G_(1), neg30[:, 0:4]))
            em("v", lambda e: e.tensor_add(G_(3), G_(3), G_(0)))
            em("v", lambda e: e.tensor_reduce(C_(0), G_(3), AX.X, ALU.max))
            em("v", lambda e: e.tensor_single_scalar(G_(2), G_(3), C_(0), ALU.is_ge))
            em("v", lambda e: e.tensor_mul(G_(2), G_(2), i4m[:]))
            em("v", lambda e: e.tensor_add(G_(2), G_(2), c4[:]))
            em("v", lambda e: e.tensor_reduce(C_(1), G_(2), AX.X, ALU.min))
            em("v", lambda e: e.tensor_single_scalar(G_(2), iota4[:], C_(1), ALU.is_equal))
            em("v", lambda e: e.tensor_add(G_(0), G_(1), G_(2)))
            for j in range(4):
                em("v", lambda e, j=j: e.tensor_copy(
                    S_(2).rearrange("p (g j) -> p g j", j=4)[:, :, j:j + 1], G_(0)))
            em("v", lambda e: e.tensor_mul(S_(3), S_(1), S_(2)))
            em("v", lambda e: e.tensor_reduce(C_(0), S_(3), AX.X, ALU.max))
            em("v", lambda e: e.tensor_single_scalar(S_(5), S_(3), C_(0), ALU.is_ge))
            em("v", lambda e: e.tensor_mul(S_(5), S_(5), i16m[:]))
            em("v", lambda e: e.tensor_add(S_(5), S_(5), c16s[:]))
            em("v", lambda e: e.tensor_reduce(C_(1), S_(5), AX.X, ALU.min))
            em("v", lambda e: e.tensor_single_scalar(S_(6), iota16[:], C_(1), ALU.is_equal))
            em("v", lambda e: e.tensor_mul(S_(4), S_(6), neg30[:]))
            em("v", lambda e: e.tensor_add(S_(4), S_(4), S_(3)))
            em("v", lambda e: e.tensor_reduce(C_(0), S_(4), AX.X, ALU.max))
            em("v", lambda e: e.tensor_single_scalar(S_(5), S_(4), C_(0), ALU.is_ge))
            em("v", lambda e: e.tensor_mul(S_(5), S_(5), i16m[:]))
            em("v", lambda e: e.tensor_add(S_(5), S_(5), c16s[:]))
            em("v", lambda e: e.tensor_reduce(C_(1), S_(5), AX.X, ALU.min))
            em("v", lambda e: e.tensor_single_scalar(S_(7), iota16[:], C_(1), ALU.is_equal))
            em("v", lambda e: e.tensor_mul(S_(5), S_(0), S_(6)))
            em("v", lambda e: e.tensor_reduce(C_(2), S_(5), AX.X, ALU.add))
            em("v", lambda e: e.tensor_mul(S_(5), S_(0), S_(7)))
            em("v", lambda e: e.tensor_reduce(C_(3), S_(5), AX.X, ALU.add))
            em("v", lambda e: e.tensor_add(C_(4), C_(2), C_(3)))
            em("v", lambda e: e.tensor_scalar_add(C_(4), C_(4), 1e-20))
            em("v", lambda e: e.reciprocal(C_(5), C_(4)))
            em("v", lambda e: e.tensor_mul(C_(2), C_(2), C_(5)))
            em("v", lambda e: e.tensor_mul(C_(3), C_(3), C_(5)))
            em("v", lambda e: e.tensor_single_scalar(S_(8), S_(6), C_(2), ALU.mult))
            em("v", lambda e: e.scalar_tensor_tensor(S_(8), S_(7), C_(3), S_(8),
                                                     ALU.mult, ALU.add),
               incs=[("pV", 1)])
            em("t", lambda e, tf=tf: e.transpose(tf[0:16, 0:128], S_(8), idf[:]),
               waits=[("pV", P.n("pV")), ("a2r", P.n("a2r"))], incs=[("pT", 1)])
            em("v", lambda e, ti=ti, tf=tf: e.tensor_copy(
                cwT[0:16, ti * 128:(ti + 1) * 128], tf[0:16, 0:128]),
               waits=[("pT", P.n("pT"))], incs=[("cwe", 1)])

        def emit_ln_core(ti, do_router, lp, xt, yscr, ybf):
            em("v", lambda e: e.tensor_reduce(col[:, 0:1], xt[:], AX.X, ALU.add),
               waits=[("dxt", P.n("dxt"))])
            em("s", lambda e: e.activation(ybf[:], xt[:], AF.Square,
                                           accum_out=col[:, 1:2]),
               waits=[("dxt", P.n("dxt")), ("pV", P.n("pV"))], incs=[("pS", 1)])
            em("v", lambda e: e.tensor_scalar_mul(col[:, 2:3], col[:, 0:1], 1.0 / D))
            em("v", lambda e: e.tensor_mul(col[:, 3:4], col[:, 2:3], col[:, 2:3]),
               waits=[("pS", P.n("pS"))])
            em("v", lambda e: e.tensor_scalar_mul(col[:, 4:5], col[:, 1:2], 1.0 / D))
            em("v", lambda e: e.tensor_sub(col[:, 4:5], col[:, 4:5], col[:, 3:4]),
               incs=[("pV", 1)])
            em("s", lambda e: e.activation(col[:, 5:6], col[:, 4:5], AF.Sqrt,
                                           bias=epscol[:, 0:1]),
               waits=[("pV", P.n("pV"))], incs=[("pS", 1)])
            em("v", lambda e: e.reciprocal(col[:, 6:7], col[:, 5:6]),
               waits=[("pS", P.n("pS"))])
            em("v", lambda e: e.tensor_scalar_mul(col[:, 7:8], col[:, 2:3], -1.0))
            em("v", lambda e: e.tensor_scalar(yscr[:], xt[:], col[:, 7:8],
                                              col[:, 6:7], ALU.add, ALU.mult))
            em("v", lambda e: e.tensor_copy(ybf[:], yscr[:]), incs=[("pV", 1)])
            for c in range(NCH):
                pp = lp["ga"] if (c % 2 == 0) else lp["gb"]
                em("t", lambda e, c=c, pp=pp: e.transpose(
                    pp[:, 0:128], ybf[:, c * 128:(c + 1) * 128], idbf[:]),
                   waits=[("pV", P.n("pV")), ("a2", max(0, P.n("a2") - 1))],
                   incs=[("a1", 1)])
                em("v", lambda e, c=c, ti=ti, pp=pp: e.tensor_copy(
                    yfm[:, c * T + ti * 128: c * T + (ti + 1) * 128], pp[:, 0:128]),
                   waits=[("a1", P.n("a1"))], incs=[("a2", 1)])
            if do_router:
                tf = lp["tf"]
                for c in range(NCH):
                    pp = tf[:, 0:128] if (c % 2 == 0) else tf[:, 512:640]
                    sl = rsc[:, (c % 2) * 128:((c % 2) + 1) * 128]
                    em("t", lambda e, c=c, pp=pp: e.transpose(
                        pp, yscr[:, c * 128:(c + 1) * 128], idf[:]),
                       waits=[("a2r", max(0, P.n("a2r") - 1))], incs=[("a1r", 1)])
                    em("v", lambda e, sl=sl, pp=pp: e.tensor_copy(sl, pp),
                       waits=[("a1r", P.n("a1r"))], incs=[("a2r", 1)])
                    w0 = [("a2r", P.n("a2r"))]
                    if c == 0:
                        w0.append(("pS", P.n("pS")))
                    em("t", lambda e, c=c, sl=sl: e.matmul(
                        lp["lg"][:, 0:16], sl, rwts[:, c * 16:(c + 1) * 16],
                        start=(c == 0), stop=(c == NCH - 1)),
                       waits=w0, incs=([("pT", 1)] if c == NCH - 1 else []))
                topk_emit(ti, lp)

        def ln_phase(src_fn, dst_dram, do_router, add_facc, facc_t, name,
                     store_out_only=False):
            lnS = ExitStack()
            lp = {"ga": PS(lnS, name + "_ga", [128, 1024], BF16),
                  "gb": PS(lnS, name + "_gb", [128, 1024], BF16),
                  "tf": PS(lnS, name + "_tf", [128, 1024], F32),
                  "lg": PS(lnS, name + "_lg", [128, 512], F32)}
            xt = SB(lnS, name + "_xt", [128, D], F32)
            yscr = SB(lnS, name + "_ys", [128, D], F32)
            ybf = SB(lnS, name + "_yb", [128, D], BF16)
            for ti in range(NT):
                em("sy", lambda e, ti=ti: e.dma_start(xt[:], src_fn(ti)),
                   waits=[("pV", P.n("pV")), ("pS", P.n("pS")), ("dxt", P.n("dxt"))],
                   incs=[("dxt", 16)])
                if add_facc:
                    for c in range(NCH):
                        pp = lp["tf"][:, 0:128] if (c % 2 == 0) else lp["tf"][:, 512:640]
                        em("t", lambda e, c=c, ti=ti, pp=pp: e.transpose(
                            pp,
                            facc_t[:, c * T + ti * 128: c * T + (ti + 1) * 128], idf[:]),
                           waits=[("pV", P.n("pV"))], incs=[("pT", 1)])
                        em("v", lambda e, c=c, pp=pp: e.scalar_tensor_tensor(
                            xt[:, c * 128:(c + 1) * 128], pp, 0.5,
                            xt[:, c * 128:(c + 1) * 128], ALU.mult, ALU.add),
                           waits=[("pT", P.n("pT")), ("dxt", P.n("dxt"))], incs=[("pV", 1)])
                if not store_out_only:
                    emit_ln_core(ti, do_router, lp, xt, yscr, ybf)
                if dst_dram is not None:
                    em("sy", lambda e, ti=ti: e.dma_start(
                        dst_dram[ti * 128:(ti + 1) * 128, :], xt[:]),
                       waits=[("pV", P.n("pV")), ("pS", P.n("pS"))], incs=[("dxt", 16)])
            P.wait("sy", "dxt")
            P.play(name)
            lnS.close()

        def shared_phase(facc_t, hab, name):
            shS = ExitStack()
            pU = [PS(shS, name + "_pU%d" % i, [128, 1024], F32) for i in range(2)]
            pDn = [PS(shS, name + "_pDn%d" % i, [128, 512], F32) for i in range(2)]
            hrl = [name_scr["hrl"], name_scr["hrl"]]
            for th in range(2):
                to = th * 512
                for mp in range(16):
                    b = mp % 2
                    wcd = wstream(b, [(su_d[:, mp * 256:(mp + 1) * 256]
                                       .rearrange("(k p) c -> p k c", p=128), 0, 4096)])
                    for sub in range(2):
                        m = mp * 2 + sub
                        pp = pU[m % 2]
                        hh = hrl[m % 2]
                        for k in range(NCH):
                            w0 = [wcd, ("rl", max(0, P.n("rl") - 1))] if k == 0 else []
                            em("t", lambda e, b=b, k=k, to=to, pp=pp, sub=sub: e.matmul(
                                pp[:, 0:512],
                                stb[b][:, k * 256 + sub * 128: k * 256 + sub * 128 + 128],
                                yfm[:, k * T + to: k * T + to + 512],
                                start=(k == 0), stop=(k == NCH - 1)),
                               waits=w0, incs=([("pT", 1)] if k == NCH - 1 else []))
                        em("s", lambda e, pp=pp, hh=hh: e.activation(
                            hh[:, 0:512], pp[:, 0:512], AF.Relu),
                           waits=[("pT", P.n("pT")), ("sq", P.n("sq"))],
                           incs=[("rl", 1)])
                        wm = [("rl", P.n("rl"))]
                        if th == 1 and m == 0:
                            wm.append(("pT", P.n("pT")))
                        em("v", lambda e, m=m, hh=hh: e.tensor_mul(
                            hab[:, m * 512:(m + 1) * 512], hh[:, 0:512], hh[:, 0:512]),
                           waits=wm, incs=[("sq", 1)])
                for mf in range(NCH):
                    b = mf % 2
                    pp = pDn[b]
                    wx = [("sq", P.n("sq"))] if mf == 0 else []
                    wcd = wstream(b, [(sd_d[:, mf * 128:(mf + 1) * 128]
                                       .rearrange("(k p) c -> p k c", p=128), 0, 4096)],
                                  wait_extra=wx)
                    for k in range(32):
                        w0 = [wcd, ("fadd", max(0, P.n("fadd") - 1))] if k == 0 else []
                        if k == 0 and mf == 0:
                            w0.append(("sq", P.n("sq")))
                        em("t", lambda e, b=b, k=k, pp=pp: e.matmul(
                            pp[:], stb[b][:, k * 128:(k + 1) * 128],
                            hab[:, k * 512:(k + 1) * 512],
                            start=(k == 0), stop=(k == 31)),
                           waits=w0, incs=([("pT", 1)] if k == 31 else []))
                    em("v", lambda e, mf=mf, to=to, pp=pp: e.scalar_tensor_tensor(
                        facc_t[:, mf * T + to: mf * T + to + 512], pp[:], 1.0,
                        facc_t[:, mf * T + to: mf * T + to + 512], ALU.mult, ALU.add),
                       waits=[("pT", P.n("pT"))], incs=[("fadd", 1)])
            P.wait("sy", "dmab0")
            P.wait("sy", "dmab1")
            P.play(name)
            shS.close()

        def moe_phase(facc_t, hab, name):
            moS = ExitStack()
            pU = [PS(moS, name + "_pU%d" % i, [128, 1024], F32) for i in range(2)]
            pDn = [PS(moS, name + "_pDn%d" % i, [128, 1024], F32) for i in range(2)]
            hrl = [name_scr["hrl"], None]
            cwb = [hab[:, 10240:11264], hab[:, 9216:10240]]
            eyeexp = hab[0:16, 11264:13312]
            ae = hab
            em("sy", lambda e: e.dma_start(eyeexp, c_eyeexp),
               waits=[("sq", P.n("sq"))], incs=[("dey", 16)])
            P.wait("t", "dey")

            def emit_bcast(ex):
                for half in range(2):
                    em("t", lambda e, ex=ex, half=half: e.matmul(
                        pU[0][:, 0:512], hab[0:16, 11264 + ex * 128: 11264 + (ex + 1) * 128],
                        cwT[:, half * 512:(half + 1) * 512], start=True, stop=True),
                       waits=[("cwcp", P.n("cwcp")), ("cwe", P.n("cwe")),
                              ("rl", P.n("rl"))],
                       incs=[("a1", 1)])
                    em("v", lambda e, ex=ex, half=half: e.tensor_copy(
                        cwb[ex % 2][:, half * 512:(half + 1) * 512], pU[0][:, 0:512]),
                       waits=[("a1", P.n("a1"))], incs=[("cwcp", 1)])

            emit_bcast(0)
            for ex in range(E):
                for mp in range(4):
                    b = mp % 2
                    wcd = wstream(b, [(wup[ex, :, mp * 256:(mp + 1) * 256]
                                       .rearrange("(k p) c -> p k c", p=128), 0, 4096)])
                    for sub in range(2):
                        m = mp * 2 + sub
                        pp = pU[m % 2]
                        hh = hrl[0][:, 0:1024] if m % 2 == 0 else hab[:, 8192:9216]
                        for k in range(NCH):
                            w0 = [wcd, ("rl", max(0, P.n("rl") - 1))] if k == 0 else []
                            if k == 0 and m == 0:
                                w0.append(("cwcp", P.n("cwcp")))
                            em("t", lambda e, b=b, k=k, pp=pp, sub=sub: e.matmul(
                                pp[:, 0:512],
                                stb[b][:, k * 256 + sub * 128: k * 256 + sub * 128 + 128],
                                yfm[:, k * T: k * T + 512],
                                start=(k == 0), stop=(k == NCH - 1)), waits=w0)
                            em("t", lambda e, b=b, k=k, pp=pp, sub=sub: e.matmul(
                                pp[:, 512:1024],
                                stb[b][:, k * 256 + sub * 128: k * 256 + sub * 128 + 128],
                                yfm[:, k * T + 512: k * T + 1024],
                                start=(k == 0), stop=(k == NCH - 1)),
                               incs=([("pT", 1)] if k == NCH - 1 else []))
                        em("s", lambda e, pp=pp, hh=hh: e.activation(hh, pp[:],
                                                                     AF.Relu),
                           waits=[("pT", P.n("pT")), ("sq", max(0, P.n("sq") - 1))],
                           incs=[("rl", 1)])
                        em("v", lambda e, hh=hh: e.tensor_mul(hh, hh, hh),
                           waits=[("rl", P.n("rl")), ("cwcp", P.n("cwcp"))])
                        wm = [("pT", P.n("pT"))] if m == 0 else []
                        em("v", lambda e, m=m, hh=hh, ex=ex: e.tensor_mul(
                            ae[:, m * T:(m + 1) * T], hh, cwb[ex % 2]),
                           waits=wm, incs=[("sq", 1)])
                if ex + 1 < E:
                    emit_bcast(ex + 1)
                for mfp in range(8):
                    b = mfp % 2
                    wx = [("sq", P.n("sq"))] if mfp == 0 else []
                    wcd = wstream(b, [(wdn[ex, :, mfp * 256:(mfp + 1) * 256]
                                       .rearrange("(k p) c -> p k c", p=128), 0, 2048)],
                                  wait_extra=wx)
                    for sub in range(2):
                        mf = mfp * 2 + sub
                        pp = pDn[mf % 2]
                        for k in range(8):
                            w0 = [wcd, ("fadd", max(0, P.n("fadd") - 1))] if k == 0 else []
                            if k == 0 and mf == 0:
                                w0.append(("sq", P.n("sq")))
                            em("t", lambda e, b=b, k=k, pp=pp, sub=sub: e.matmul(
                                pp[:, 0:512],
                                stb[b][:, k * 256 + sub * 128: k * 256 + sub * 128 + 128],
                                ae[:, k * T: k * T + 512],
                                start=(k == 0), stop=(k == 7)), waits=w0)
                            em("t", lambda e, b=b, k=k, pp=pp, sub=sub: e.matmul(
                                pp[:, 512:1024],
                                stb[b][:, k * 256 + sub * 128: k * 256 + sub * 128 + 128],
                                ae[:, k * T + 512: k * T + 1024],
                                start=(k == 0), stop=(k == 7)),
                               incs=([("pT", 1)] if k == 7 else []))
                        em("v", lambda e, mf=mf, pp=pp: e.scalar_tensor_tensor(
                            facc_t[:, mf * T: mf * T + 1024], pp[:], 1.0,
                            facc_t[:, mf * T: mf * T + 1024], ALU.mult, ALU.add),
                           waits=[("pT", P.n("pT"))], incs=[("fadd", 1)])
            P.wait("sy", "dmab0")
            P.wait("sy", "dmab1")
            P.play(name)
            moS.close()

        name_scr = {}

        ln_phase(lambda ti: x_in[ti * 128:(ti + 1) * 128, :], None, True, False,
                 None, "ln1")

        with ExitStack() as ffn1:
            facc = SB(ffn1, "facc1", [128, NCH * T], F32)
            hab = SB(ffn1, "hab1", [128, 16384], BF16)
            name_scr["hrl"] = SB(ffn1, "hrl1", [128, 1024], BF16)

            em("v", lambda e: e.memset(facc[:], 0.0), incs=[("fz", 1)])
            P.wait("t", "fz")
            P.play("fz1")
            shared_phase(facc, hab, "sh1")
            moe_phase(facc, hab, "moe1")
            ln_phase(lambda ti: x_in[ti * 128:(ti + 1) * 128, :], xbuf, False, True,
                     facc, "upd1")

        rg8 = [[0, 1, 2, 3, 4, 5, 6, 7]]
        with ExitStack() as attnO:
            ctxn = SB(attnO, "ctxn", [128, 16384], BF16)
            qbrow = SB(attnO, "qbrow", [1, 128], F32)
            kvS2 = ExitStack()
            Kloc = SB(kvS2, "Kloc", [128, 4096], BF16)
            Vloc = SB(kvS2, "Vloc", [128, 4160], BF16)
            with ExitStack() as qkvS:
                qC = PS(qkvS, "qC", [128, 512], F32)
                qD = PS(qkvS, "qD", [128, 512], F32)
                qE = PS(qkvS, "qE", [128, 512], F32)
                qF = PS(qkvS, "qF", [128, 512], F32)
                wvtb = SB(qkvS, "wvtb", [128, 8192], BF16)
                wktb = SB(qkvS, "wktb", [128, 8192], BF16)
                qstg = [SB(qkvS, "qstg%d" % i, [128, 1024], BF16) for i in range(2)]
                sqb = SB(qkvS, "sqb", [128, 1024], BF16)
                for j in range(4):
                    wstream(j % 2, [(wvt[j * 512:(j + 1) * 512, :]
                                     .rearrange("(k p) n -> p k n", p=128), j * 2048, 2048)],
                            dst=wvtb)
                for j in range(4):
                    wstream(j % 2, [(wkt[j * 512:(j + 1) * 512, :]
                                     .rearrange("(k p) n -> p k n", p=128), j * 2048, 2048)],
                            dst=wktb)
                wk_done = [("sc0", P.n("sc0")), ("sc1", P.n("sc1"))]
                em("v", lambda e: e.memset(Vloc[:], 1.0))
                for t_ in range(NT):
                    for k in range(NCH):
                        w0 = (wk_done + [("vev", P.n("vev"))]) if k == 0 else []
                        em("t", lambda e, t_=t_, k=k: e.matmul(
                            qD[:], yfm[:, k * T + t_ * 128: k * T + (t_ + 1) * 128],
                            wvtb[:, k * 512:(k + 1) * 512],
                            start=(k == 0), stop=(k == NCH - 1)),
                           waits=w0, incs=([("pT", 1)] if k == NCH - 1 else []))
                    em("v", lambda e, t_=t_: e.tensor_copy(
                        Vloc[:, t_ * 520:(t_ + 1) * 520]
                        .rearrange("p (kv c) -> p kv c", c=65)[:, :, 0:64],
                        qD[:].rearrange("p (kv c) -> p kv c", c=64)),
                       waits=[("pT", P.n("pT"))], incs=[("vev", 1)])
                for Pp in range(4):
                    for n_ in range(2):
                        for k in range(NCH):
                            w0 = [("kev", P.n("kev")), ("vev", P.n("vev"))] if k == 0 else []
                            em("t", lambda e, Pp=Pp, n_=n_, k=k: e.matmul(
                                qC[:], wktb[:, k * 512 + Pp * 128: k * 512 + (Pp + 1) * 128],
                                yfm[:, k * T + n_ * 512: k * T + n_ * 512 + 512],
                                start=(k == 0), stop=(k == NCH - 1)),
                               waits=w0, incs=([("pT", 1)] if k == NCH - 1 else []))
                        em("v", lambda e, Pp=Pp, n_=n_: e.tensor_copy(
                            Kloc[:, Pp * 1024 + n_ * 512: Pp * 1024 + n_ * 512 + 512], qC[:]),
                           waits=[("pT", P.n("pT"))], incs=[("kev", 1)])
                for n_ in range(2):
                    pk = qF if n_ == 0 else qE
                    for Pp in range(4):
                        em("s", lambda e, Pp=Pp, n_=n_: e.activation(
                            sqb[:, 0:512],
                            Kloc[:, Pp * 1024 + n_ * 512: Pp * 1024 + n_ * 512 + 512],
                            AF.Square),
                           waits=[("kev", P.n("kev")), ("knm", P.n("knm"))],
                           incs=[("ksq", 1)])
                        em("t", lambda e, pk=pk, Pp=Pp: e.matmul(
                            pk[0:1, 0:512], onesb[:, 0:1], sqb[:, 0:512],
                            start=(Pp == 0), stop=(Pp == 3)),
                           waits=[("ksq", P.n("ksq"))], incs=[("knm", 1)])
                em("v", lambda e: e.tensor_reduce(col[0:1, 8:9], qF[0:1, 0:512], AX.X, ALU.max),
                   waits=[("knm", P.n("knm"))])
                em("v", lambda e: e.tensor_reduce(col[0:1, 9:10], qE[0:1, 0:512], AX.X, ALU.max))
                em("v", lambda e: e.tensor_max(col[0:1, 10:11], col[0:1, 8:9], col[0:1, 9:10]),
                   incs=[("knr", 1)])
                em("s", lambda e: e.activation(col[0:1, 11:12], col[0:1, 10:11], AF.Sqrt),
                   waits=[("knr", P.n("knr"))], incs=[("kns", 1)])
                em("v", lambda e: e.memset(tk[0:1, 0:8], 0.0))
                em("v", lambda e: e.tensor_copy(tk[0:1, 0:1], col[0:1, 11:12]),
                   waits=[("kns", P.n("kns"))], incs=[("kn8", 1)])
                for Pp in range(4):
                    for h in range(4):
                        idx = Pp * 4 + h
                        b = idx % 2
                        wcd = wstream(b, [(wqt[:, idx * 128:(idx + 1) * 128]
                                           .rearrange("(k p) c -> p k c", p=128), 0, 2048)])
                        for n_ in range(2):
                            for k in range(NCH):
                                w0 = [wcd, ("qev", P.n("qev")),
                                      ("knm", P.n("knm"))] if k == 0 else []
                                em("t", lambda e, b=b, k=k, n_=n_: e.matmul(
                                    qC[:], stb[b][:, k * 128:(k + 1) * 128],
                                    yfm[:, k * T + n_ * 512: k * T + n_ * 512 + 512],
                                    start=(k == 0), stop=(k == NCH - 1)),
                                   waits=w0, incs=([("pT", 1)] if k == NCH - 1 else []))
                            em("v", lambda e, idx=idx, n_=n_: e.tensor_copy(
                                qstg[idx % 2][:, n_ * 512:(n_ + 1) * 512], qC[:]),
                               waits=[("pT", P.n("pT")),
                                      ("dq%d" % (idx % 2), P.n("dq%d" % (idx % 2)))],
                               incs=[("qev", 1)])
                        em("s", lambda e, idx=idx: e.activation(
                            sqb[:], qstg[idx % 2][:], AF.Square),
                           waits=[("qev", P.n("qev")), ("qnm", P.n("qnm"))],
                           incs=[("qsq", 1)])
                        for n_ in range(2):
                            pk = qF if n_ == 0 else qE
                            em("t", lambda e, pk=pk, n_=n_: e.matmul(
                                pk[0:1, 0:512], onesb[:, 0:1],
                                sqb[:, n_ * 512:(n_ + 1) * 512],
                                start=True, stop=True),
                               waits=[("qsq", P.n("qsq")), ("qnr", P.n("qnr"))],
                               incs=[("qnm", 1)])
                            em("v", lambda e, pk=pk, idx=idx, n_=n_: e.tensor_reduce(
                                qbrow[0:1, idx * 8 + n_ * 4: idx * 8 + n_ * 4 + 4],
                                pk[0:1, 0:512].rearrange("p (s j) -> p s j", j=128),
                                AX.X, ALU.max),
                               waits=[("qnm", P.n("qnm"))], incs=[("qnr", 1)])
                        em("sy", lambda e, Pp=Pp, h=h, idx=idx: e.dma_start(
                            qdr[Pp, :, h * 1024:(h + 1) * 1024], qstg[idx % 2][:]),
                           waits=[("qev", P.n("qev")), ("qsq", P.n("qsq"))],
                           incs=[("dq%d" % (idx % 2), 16)])
                em("sy", lambda e: e.dma_start(
                    kag_i.rearrange("(Pp p) t -> p Pp t", p=128),
                    Kloc[:].rearrange("p (Pp t) -> p Pp t", t=1024)),
                   waits=[("kev", P.n("kev"))], incs=[("dag", 16)])
                em("sy", lambda e: e.dma_start(
                    vag_i.rearrange("(s p) c -> p s c", p=128),
                    Vloc[:].rearrange("p (s c) -> p s c", c=520)),
                   waits=[("vev", P.n("vev"))], incs=[("dag", 16)])
                em("sy", lambda e: e.dma_start(nrm_i, tk[0:1, 0:8]),
                   waits=[("kn8", P.n("kn8"))], incs=[("dag", 16)])
                P.wait("g", "dag")
                em("g", lambda e: e.collective_compute(
                    "AllGather", ALU.bypass, rg8, [kag_i.opt()], [kag_o.opt()]),
                   incs=[("ccs", 1)], dma=True)
                em("g", lambda e: e.collective_compute(
                    "AllGather", ALU.bypass, rg8, [vag_i.opt()], [vag_o.opt()]),
                   incs=[("ccs", 1)], dma=True)
                em("g", lambda e: e.collective_compute(
                    "AllGather", ALU.bypass, rg8, [nrm_i.opt()], [nrm_o.opt()]),
                   incs=[("ccs", 1)], dma=True)
                P.wait("sy", "dq0")
                P.wait("sy", "dq1")
                P.wait("sy", "dmab0")
                P.wait("sy", "dmab1")
                P.play("qkv")

            with ExitStack() as passS:
                Kp = SB(passS, "Kp", [128, 8192], BF16)
                Vp = SB(passS, "Vp", [128, 64 * 130], BF16)
                Qp = SB(passS, "Qp", [128, 4096], BF16)
                atb = SB(passS, "atb", [128, 5 * 512], BF16)
                escr = SB(passS, "escr", [128, 1536], F32)
                BT = SB(passS, "BT", [128, 2080], BF16)
                trils = SB(passS, "trils", [128, 512], BF16)
                pads = SB(passS, "pads", [1, 8 * 65], BF16)
                sS = [PS(passS, "sS%d" % i, [128, 512], F32) for i in range(4)]
                cC = PS(passS, "cC", [128, 512], F32)
                cD = PS(passS, "cD", [128, 512], F32)
                bX = PS(passS, "bX", [128, 512], F32)
                BTr = BT
                em("sy", lambda e: e.dma_start(trils[:], c_tril), incs=[("dbt", 16)])
                em("sy", lambda e: e.dma_start(pads[:], c_pad), incs=[("dbt", 16)])
                P.wait("v", "ccs")
                em("sy", lambda e: e.dma_start(
                    tk[0:1, 16:24],
                    nrm_o[:, 0:1].rearrange("r c -> c r"), allow_slow_non_contiguous=True),
                   waits=[("ccs", P.n("ccs"))], incs=[("dbt", 16)])
                em("v", lambda e: e.tensor_reduce(col[0:1, 12:13], tk[0:1, 16:24],
                                                  AX.X, ALU.max),
                   waits=[("dbt", P.n("dbt"))])
                em("s", lambda e: e.activation(qbrow[:], qbrow[:], AF.Sqrt),
                   waits=[("qnr", P.n("qnr"))], incs=[("bts", 1)])
                for Pp in range(4):
                    for s_ in range(8):
                        em("v", lambda e, Pp=Pp, s_=s_: e.tensor_reduce(
                            tk[0:1, 32 + Pp * 8 + s_: 33 + Pp * 8 + s_],
                            bass.AP(qbrow, Pp * 32 + s_, [[128, 1], [8, 4]]),
                            AX.X, ALU.max),
                           waits=([("bts", P.n("bts"))] if (Pp == 0 and s_ == 0) else []))
                em("v", lambda e: e.tensor_single_scalar(
                    tk[0:1, 64:96], tk[0:1, 32:64], col[0:1, 12:13], ALU.mult))
                em("v", lambda e: e.tensor_scalar_mul(tk[0:1, 64:96],
                                                      tk[0:1, 64:96], -0.125),
                   incs=[("btc", 1)])
                for Pp in range(4):
                    for s_ in range(8):
                        em("v", lambda e, Pp=Pp, s_=s_: e.tensor_single_scalar(
                            BTr[0:1, (Pp * 8 + s_) * 65: (Pp * 8 + s_ + 1) * 65],
                            pads[0:1, s_ * 65:(s_ + 1) * 65],
                            tk[0:1, 64 + Pp * 8 + s_: 65 + Pp * 8 + s_], ALU.add),
                           incs=([("btr", 1)] if (Pp == 3 and s_ == 7) else []))
                for j in range(5):
                    w = min(512, 2080 - j * 512)
                    em("t", lambda e, j=j, w=w: e.matmul(
                        bX[:, 0:w], onesbr[0:1, 0:128], BTr[0:1, j * 512: j * 512 + w],
                        start=True, stop=True),
                       waits=[("btr", P.n("btr")), ("btv", P.n("btv"))], incs=[("btm", 1)])
                    em("v", lambda e, j=j, w=w: e.tensor_copy(
                        BT[:, j * 512: j * 512 + w], bX[:, 0:w]),
                       waits=[("btm", P.n("btm"))], incs=[("btv", 1)])
                for Pp in range(4):
                    wP = [("avd", P.n("avd")), ("epv", P.n("epv"))]
                    em("sy", lambda e, Pp=Pp: e.dma_start(Qp[:], qdr[Pp, :, :]),
                       waits=wP + [("dkp", P.n("dkp"))], incs=[("dkp", 16)])
                    KpV = Kp[:].rearrange("p (s c j) -> p c s j", s=8, c=8, j=128)
                    KAGv = kag_o.rearrange("(c Pq p) (s j) -> c Pq p s j",
                                           c=8, Pq=4, j=128)
                    for cc in range(8):
                        em("sy", lambda e, cc=cc, Pp=Pp, KpV=KpV, KAGv=KAGv: e.dma_start(
                            KpV[:, cc:cc + 1, 0:4, :],
                            KAGv[cc:cc + 1, Pp:Pp + 1, :, 0:4, :]
                            .rearrange("c Pq p s j -> p (c Pq) s j")),
                           waits=([("dkp", P.n("dkp"))] if cc == 0 else []),
                           incs=[("dkp", 16)])
                        em("sy", lambda e, cc=cc, Pp=Pp, KpV=KpV, KAGv=KAGv: e.dma_start(
                            KpV[:, 7 - cc:8 - cc, 4:8, :],
                            KAGv[cc:cc + 1, Pp:Pp + 1, :, 4:8, :]
                            .rearrange("c Pq p s j -> p (c Pq) s j")),
                           incs=[("dkp", 16)])
                    VpV = Vp[:].rearrange("p (s c j) -> p c s j", s=8, c=8, j=130)
                    VAGv = vag_o.rearrange("(c s p) q -> c s p q", c=8, p=128)
                    for cc in range(8):
                        em("sy", lambda e, cc=cc, Pp=Pp, VpV=VpV, VAGv=VAGv: e.dma_start(
                            VpV[:, cc:cc + 1, 0:4, :],
                            VAGv[cc:cc + 1, 0:4, :, 130 * Pp:130 * Pp + 130]
                            .transpose([2, 0, 1, 3])),
                           waits=([("dvp", P.n("dvp"))] if cc == 0 else []),
                           incs=[("dvp", 16)])
                        em("sy", lambda e, cc=cc, Pp=Pp, VpV=VpV, VAGv=VAGv: e.dma_start(
                            VpV[:, 7 - cc:8 - cc, 4:8, :],
                            VAGv[cc:cc + 1, 4:8, :, 130 * Pp:130 * Pp + 130]
                            .transpose([2, 0, 1, 3])),
                           incs=[("dvp", 16)])
                    ld_done = [("dkp", P.n("dkp")), ("dvp", P.n("dvp")),
                               ("btv", P.n("btv")), ("dbt", P.n("dbt"))]
                    for s_ in range(8):
                        nfull = LF[s_]
                        qv = Qp[:].rearrange("p (h s j) -> p h s j", h=4, j=128)
                        pts_after = {}

                        def emit_S(i, first, Pp=Pp, s_=s_, nfull=nfull, qv=qv,
                                   ld_done=ld_done, pts_after=pts_after):
                            diag = (i == nfull)
                            SA = sS[2 * (i % 2)]
                            SBp = sS[2 * (i % 2) + 1]
                            if diag:
                                lhsE = Kloc[0:64, Pp * 1024 + s_ * 128: Pp * 1024 + (s_ + 1) * 128]
                                lhsO = Kloc[64:128, Pp * 1024 + s_ * 128: Pp * 1024 + (s_ + 1) * 128]
                            else:
                                lhsE = Kp[0:64, i * 128:(i + 1) * 128]
                                lhsO = Kp[64:128, i * 128:(i + 1) * 128]
                            rhsE = qv[0:64, :, s_:s_ + 1, :]
                            rhsO = qv[64:128, :, s_:s_ + 1, :]
                            w0 = [("exd", P.n("exd"))]
                            if first:
                                w0 += ld_done + [("epv", P.n("epv"))]
                            em("t", lambda e, SA=SA, lhsE=lhsE, rhsE=rhsE: e.matmul(
                                SA[:, 0:512], lhsE, rhsE, start=True, stop=True),
                               waits=w0)
                            em("t", lambda e, SBp=SBp, lhsO=lhsO, rhsO=rhsO: e.matmul(
                                SBp[:, 0:512], lhsO, rhsO, start=True, stop=True),
                               incs=[("pTs", 1)])
                            pts_after[i] = P.n("pTs")
                            if diag:
                                em("v", lambda e, SA=SA: e.tensor_add(
                                    SA[:, 0:512], SA[:, 0:512], trils[:]),
                                   waits=[("pTs", P.n("pTs"))])
                                em("v", lambda e, SBp=SBp: e.tensor_add(
                                    SBp[:, 0:512], SBp[:, 0:512], trils[:]),
                                   incs=[("tri", 1)])

                        emit_S(0, True)
                        for i in range(nfull + 1):
                            diag = (i == nfull)
                            if not diag:
                                emit_S(i + 1, False)
                            SA = sS[2 * (i % 2)]
                            SBp = sS[2 * (i % 2) + 1]
                            bidx = (Pp * 8 + s_) * 65 + (64 if diag else i)
                            a0 = atb[:, (i % 2) * 1024: (i % 2) * 1024 + 512]
                            a1_ = atb[:, (i % 2) * 1024 + 512: (i % 2) * 1024 + 1024]
                            we = [("pTs", pts_after[i]), ("avd", max(0, P.n("avd") - 2))]
                            if diag:
                                we.append(("tri", P.n("tri")))
                            em("s", lambda e, SA=SA, a0=a0, bidx=bidx: e.activation(
                                a0, SA[:, 0:512], AF.Exp, bias=BT[:, bidx:bidx + 1],
                                scale=0.125),
                               waits=we, incs=[("exd", 1)])
                            em("s", lambda e, SBp=SBp, a1_=a1_, bidx=bidx: e.activation(
                                a1_, SBp[:, 0:512], AF.Exp, bias=BT[:, bidx:bidx + 1],
                                scale=0.125),
                               incs=[("exd", 1)])
                            if diag:
                                vE = Vloc[:, s_ * 520: s_ * 520 + 520]
                                lv0 = vE.rearrange("p (kv c) -> p kv c", c=65)[:, 2 * Pp:2 * Pp + 1, :]
                                lv1 = vE.rearrange("p (kv c) -> p kv c", c=65)[:, 2 * Pp + 1:2 * Pp + 2, :]
                            else:
                                lv0 = Vp[:, i * 130: i * 130 + 65]
                                lv1 = Vp[:, i * 130 + 65: i * 130 + 130]
                            em("t", lambda e, lv0=lv0, a0=a0, i=i, diag=diag: e.matmul(
                                cC[0:65, :], lv0, a0, start=(i == 0), stop=diag),
                               waits=[("exd", P.n("exd"))], incs=[("avd", 1)])
                            em("t", lambda e, lv1=lv1, a1_=a1_, i=i, diag=diag: e.matmul(
                                cD[0:65, :], lv1, a1_, start=(i == 0), stop=diag),
                               incs=[("avd", 1)])
                        for par in range(2):
                            ctx = cC if par == 0 else cD
                            bc = bX
                            em("s", lambda e, ctx=ctx, par=par: e.activation(
                                escr[64:65, 0:512], ctx[64:65, 0:512], AF.Ln),
                               waits=[("avd", P.n("avd")), ("ebm", P.n("ebm"))],
                               incs=[("eln", 1)])
                            em("s", lambda e: e.activation(
                                escr[64:65, 512:1024], escr[64:65, 0:512], AF.Exp,
                                scale=-1.0),
                               incs=[("erc", 1)])
                            em("t", lambda e, bc=bc: e.matmul(
                                bc[0:64, 0:512], onesf[64:65, 0:64],
                                escr[64:65, 512:1024], start=True, stop=True),
                               waits=[("erc", P.n("erc")), ("ebv", P.n("ebv"))],
                               incs=[("ebm", 1)])
                            em("v", lambda e, bc=bc: e.tensor_copy(escr[0:64, 1024:1536], bc[0:64, 0:512]),
                               waits=[("ebm", P.n("ebm"))], incs=[("ebv", 1)])
                            off = s_ * 2048 + Pp * 512
                            if par == 0:
                                em("v", lambda e, ctx=ctx, off=off: e.tensor_tensor(
                                    ctxn[0:64, off:off + 512], ctx[0:64, 0:512],
                                    escr[0:64, 1024:1536], ALU.mult),
                                   waits=[("ebv", P.n("ebv"))], incs=[("epv", 1)])
                            else:
                                em("v", lambda e, ctx=ctx: e.tensor_tensor(
                                    atb[0:64, 2048:2560], ctx[0:64, 0:512], escr[0:64, 1024:1536], ALU.mult),
                                   waits=[("ebv", P.n("ebv")),
                                          ("dcx", P.n("dcx"))], incs=[("epv", 1)])
                                em("g", lambda e, off=off: e.dma_start(
                                    ctxn[64:128, off:off + 512], atb[0:64, 2048:2560]),
                                   waits=[("epv", P.n("epv"))], incs=[("dcx", 16)],
                                   dma=True)
                P.wait("g", "dcx")
                P.wait("sy", "dkp")
                P.wait("sy", "dvp")
                P.play("attn")

            kvS2.close()
            with ExitStack() as woS:
                wotb = SB(woS, "wotb", [128, 16 * 2048], BF16)
                pW0 = PS(woS, "pW0", [128, 1024], F32)
                pW1 = PS(woS, "pW1", [128, 1024], F32)
                wga = PS(woS, "wga", [128, 1024], BF16)
                wgb = PS(woS, "wgb", [128, 1024], BF16)
                wlp = {"ga": wga, "gb": wgb, "tf": pW0, "lg": pW1}
                xt = SB(woS, "wo_xt", [128, D], F32)
                yscr = SB(woS, "wo_ys", [128, D], F32)
                ybf = SB(woS, "wo_yb", [128, D], BF16)
                pso = [pW0[:, 0:512], pW0[:, 512:1024],
                       pW1[:, 0:512], pW1[:, 512:1024]]
                for c in range(NCH):
                    wstream(c % 2, [(wot[c * 128:(c + 1) * 128, :]
                                     .rearrange("p (k c2) -> p k c2", c2=128),
                                     c * 2048, 2048)], dst=wotb)
                wo_done = [("sc0", P.n("sc0")), ("sc1", P.n("sc1")),
                           ("dcx", P.n("dcx")), ("epv", P.n("epv"))]
                for s_ in range(8):
                    em("sy", lambda e, s_=s_: e.dma_start(
                        xt[:], xbuf[s_ * 128:(s_ + 1) * 128, :]),
                       waits=[("pV", P.n("pV")), ("pS", P.n("pS")),
                              ("dxt", P.n("dxt"))],
                       incs=[("dxt", 16)])
                    for j in range(4):
                        for c in range(NCH):
                            w0 = (wo_done + [("pV", P.n("pV")), ("cwe", P.n("cwe")),
                                             ("a2r", P.n("a2r"))]) if c == 0 else []
                            em("t", lambda e, s_=s_, j=j, c=c: e.matmul(
                                pso[j], ctxn[:, s_ * 2048 + c * 128: s_ * 2048 + (c + 1) * 128],
                                wotb[:, c * 2048 + j * 512: c * 2048 + (j + 1) * 512],
                                start=(c == 0), stop=(c == NCH - 1)),
                               waits=w0, incs=([("pT", 1)] if c == NCH - 1 else []))
                    for j in range(4):
                        em("v", lambda e, j=j: e.scalar_tensor_tensor(
                            xt[:, j * 512:(j + 1) * 512], pso[j], 1.0,
                            xt[:, j * 512:(j + 1) * 512], ALU.mult, ALU.add),
                           waits=([("pT", P.n("pT")), ("dxt", P.n("dxt"))] if j == 0 else []),
                           incs=([("pV", 1)] if j == 3 else []))
                    emit_ln_core(s_, True, wlp, xt, yscr, ybf)
                    em("sy", lambda e, s_=s_: e.dma_start(
                        xbuf[s_ * 128:(s_ + 1) * 128, :], xt[:]),
                       waits=[("pV", P.n("pV")), ("pS", P.n("pS"))], incs=[("dxt", 16)])
                P.wait("sy", "dxt")
                P.wait("sy", "dmab0")
                P.wait("sy", "dmab1")
                P.play("wo")

        with ExitStack() as ffn2:
            facc = SB(ffn2, "facc2", [128, NCH * T], F32)
            hab = SB(ffn2, "hab2", [128, 16384], BF16)
            name_scr["hrl"] = SB(ffn2, "hrl2", [128, 1024], BF16)

            em("v", lambda e: e.memset(facc[:], 0.0))
            P.play("fz2")
            shared_phase(facc, hab, "sh2")
            moe_phase(facc, hab, "moe2")
            ln_phase(lambda ti: xbuf[ti * 128:(ti + 1) * 128, :], out_d, False, True,
                     facc, "fin", store_out_only=True)

    return nc



def _consts():
    eye = np.eye(128, dtype=np.float32)
    eyeexp = np.zeros((16, E * 128), dtype=np.float32)
    for e in range(E):
        eyeexp[e, e * 128:(e + 1) * 128] = 1.0
    kk = np.arange(128)[:, None]
    qq = np.arange(128)[None, :]
    blk = np.where(kk > qq, NEG, 0.0).astype(np.float32)
    return {
        "c_idbf": eye.astype(bf16), "c_idf": eye,
        "c_ones": np.ones((128, 128), np.float32),
        "c_iota16": np.tile(np.arange(16, dtype=np.float32), (128, 1)),
        "c_iota4": np.tile(np.arange(4, dtype=np.float32), (128, 1)),
        "c_i16m": np.tile(np.arange(16, dtype=np.float32) - 16.0, (128, 1)),
        "c_i4m": np.tile(np.arange(4, dtype=np.float32) - 4.0, (128, 1)),
        "c_onesb": np.ones((128, 1), bf16),
        "c_onesbr": np.ones((1, 128), bf16),
        "c_eyeexp": eyeexp.astype(bf16),
        "c_tril": np.tile(blk, (1, 4)).astype(bf16),
    }


def _pad_table(c):
    pad = np.zeros((1, 8 * 65), dtype=np.float32)
    for s in range(8):
        g = g_of(c, s)
        pad[0, s * 65 + g: s * 65 + 64] = NEG
    return pad.astype(bf16)


def _shard_x(x):
    x = np.asarray(x).reshape(8192, D)
    return [np.ascontiguousarray(
        np.concatenate([x[g * 128:(g + 1) * 128] for g in core_tiles(c)], axis=0))
        for c in range(8)]


def _perm_qorder():
    order = []
    for Pp in range(4):
        for h in range(4):
            order.extend(range((8 * Pp + h) * 64, (8 * Pp + h) * 64 + 64))
            order.extend(range((8 * Pp + 4 + h) * 64, (8 * Pp + 4 + h) * 64 + 64))
    return np.array(order)


def make_inputs(L0_attn_x, L0_attn_Wq, L0_attn_Wk, L0_attn_Wv, L0_attn_Wo,
                L0_ffn_router_weight, L0_ffn_e_bias, L0_ffn_w_up, L0_ffn_w_down,
                L0_ffn_shared_up, L0_ffn_shared_down):
    order = _perm_qorder()
    base = {
        "wqt": np.ascontiguousarray(np.asarray(L0_attn_Wq, np.float32).T[:, order]),
        "wkt": np.ascontiguousarray(np.asarray(L0_attn_Wk, np.float32).T),
        "wvt": np.ascontiguousarray(np.asarray(L0_attn_Wv, np.float32).T),
        "wot": np.ascontiguousarray(np.asarray(L0_attn_Wo, np.float32).T[order, :]),
        "rwt": np.ascontiguousarray(np.asarray(L0_ffn_router_weight, np.float32).T),
        "ebias": np.ascontiguousarray(np.asarray(L0_ffn_e_bias, np.float32).reshape(1, 16)),
        "wup": np.ascontiguousarray(np.asarray(L0_ffn_w_up, np.float32)),
        "wdn": np.ascontiguousarray(np.asarray(L0_ffn_w_down, np.float32)),
        "su": np.ascontiguousarray(np.asarray(L0_ffn_shared_up, np.float32)),
        "sd": np.ascontiguousarray(np.asarray(L0_ffn_shared_down, np.float32)),
    }
    base.update(_consts())
    xs = _shard_x(L0_attn_x)
    in_maps = []
    for c in range(8):
        m = dict(base)
        m["x0"] = xs[c]
        m["c_pad"] = _pad_table(c)
        in_maps.append(m)
    return in_maps


def kernel(**inputs):
    in_maps = make_inputs(**inputs)
    nc = build()
    res = run_bass_kernel_spmd(nc, in_maps, core_ids=list(range(8)))
    out = np.zeros((8192, D), dtype=np.float32)
    for c in range(8):
        oc = res.results[c]["out"]
        for s, g in enumerate(core_tiles(c)):
            out[g * 128:(g + 1) * 128] = oc[s * 128:(s + 1) * 128]
    return out.reshape(1, 8192, D)
